# version 04 (final)
# baseline (speedup 1.0000x reference)
import contextlib
import numpy as np
import ml_dtypes
import concourse.bass as bass
import concourse.mybir as mybir
from concourse.bass_utils import run_bass_kernel_spmd

F32 = mybir.dt.float32
BF16 = mybir.dt.bfloat16
I32 = mybir.dt.int32
AF = mybir.ActivationFunctionType
ALU = mybir.AluOpType
AX = mybir.AxisListType

D = 1024
SEQ = 2048
NSEQ = 2
NTOK = NSEQ * SEQ
NT_SEQ = SEQ // 128
DC = D // 128
LN3 = 1.0986122886681098

_EB = {"v": "vector", "s": "scalar", "t": "tensor", "g": "gpsimd", "y": "sync"}


class Ctx:
    def __init__(self, nc, block):
        self.nc = nc
        self.block = block
        self.stack = contextlib.ExitStack()
        self.sem = {}
        for name in ("v", "s", "t", "g", "dml", "dms", "cc"):
            self.sem[name] = self.stack.enter_context(nc.semaphore(name))
        self.tot = {k: 0 for k in self.sem}

    _uid = 0

    def nm(self, base):
        Ctx._uid += 1
        return f"{base}_{Ctx._uid}"

    def mark(self, inst, name, n=1):
        self.tot[name] += n
        return inst.then_inc(self.sem[name], n)

    def fence(self, waiters, producers=("v", "s", "t", "g", "dml", "dms")):
        targets = [(self.sem[p], self.tot[p]) for p in producers if self.tot[p] > 0]
        for w in waiters:
            sec = getattr(self.block, _EB[w])

            @sec
            def _(eng, targets=targets):
                for sm, val in targets:
                    eng.wait_ge(sm, val)


def sect(ctx, eng):
    return getattr(ctx.block, _EB[eng])


def build_program(upto=99, stage=99, substage=99, dbg=None, only=None):
    nc = bass.Bass("TRN2", target_bir_lowering=False, debug=False,
                   detect_race_conditions=False)

    x_in = nc.dram_tensor("x", [NTOK, D], F32, kind="ExternalInput").ap()
    y_out = nc.dram_tensor("y", [NTOK, D], F32, kind="ExternalOutput").ap()

    W = {}
    def win(name, shape, dtype=F32):
        W[name] = nc.dram_tensor(name, list(shape), dtype, kind="ExternalInput").ap()

    win("ident", (128, 128), BF16)
    win("ropec", (128, 4 * SEQ), BF16)
    win("ropes", (128, 4 * SEQ), BF16)
    win("L0a_wqkv", (D, 3 * D))
    win("L0a_bqkv_pc", (128, 16))
    win("L0a_bv_bc", (128, D))
    win("L0a_gnorm_pc", (128, DC))
    win("L0a_wproj", (D, D))
    win("L0a_bproj_bc", (128, D))
    win("L0f_wq", (D, 512))
    win("L0f_wk", (D, 512))
    win("L0f_wsd", (D, 80))
    win("L0f_wv", (D, 320))
    win("L0f_wo", (384, D))
    win("cmask", (128, 128))
    win("L1a_wq", (D, D))
    win("L1a_wk", (D, D))
    win("L1a_wv", (D, D))
    win("L1a_wo", (D, D))
    win("L1f_gate", (D, 8))
    win("L1f_w1", (8, D, 512))
    win("L1f_w2", (8, 512, D))
    win("L1f_w3", (8, D, 512))
    win("L2a_win", (D, D))
    win("L2a_wout", (D, D))
    win("L2f_wup", (3072, D))
    win("L2f_wgate", (3072, D))
    win("L2f_wdown", (D, 3072))
    win("L3a_wv_p", (D, D))
    win("L3a_wqk_t", (16, 128, D))
    win("L3a_wf", (128, 64))
    win("L3a_bf_pc", (128, 1))
    win("L3a_wout", (D, D))
    win("L3a_bout_bc", (128, D))
    win("L3f_wq", (D, D))
    win("L3f_wk", (D, D))
    win("L3f_wv", (D, D))
    win("L3f_wo", (D, D))
    win("L3f_bq_pc", (128, 8))
    win("L3f_bk_pc", (128, 8))
    win("L3f_bv_bc", (128, D))
    win("L3f_bo_bc", (128, D))
    win("L3f_ln1g_bc", (128, D))
    win("L3f_ln1b_bc", (128, D))
    win("L3f_ln2g_bc", (128, D))
    win("L3f_ln2b_bc", (128, D))
    win("L3f_gate", (D, 8))
    win("L3f_gateb_bc", (128, 8))
    win("L3f_w1t", (8, 32, 128, D))
    win("L3f_b1_pc", (128, 256))
    win("L3f_w2", (8, 4096, D))
    win("L3f_b2_bc", (128, 8 * D))

    xres = nc.dram_tensor("xres", [NTOK, D], F32).ap()
    ccbufs = {
        "sc_up": nc.dram_tensor("sc_up", [NTOK, 3072], BF16).ap(),
        "sc_g": nc.dram_tensor("sc_g", [NTOK, 3072], BF16).ap(),
        "sc_g2": nc.dram_tensor("sc_g2", [NTOK, 3072], BF16).ap(),
        "sc_o": nc.dram_tensor("sc_o", [NTOK, 3072], BF16).ap(),
        "cc_in": nc.dram_tensor("cc_in", [1, 4], F32),
        "cc_out": nc.dram_tensor("cc_out", [1, 4], F32),
        "prscr": nc.dram_tensor("prscr", [1, 128], F32).ap(),
        "zres": nc.dram_tensor("zres", [NTOK, D], F32).ap(),
        "y2acc": nc.dram_tensor("y2acc", [NTOK, D], F32).ap(),
    }

    with nc.Block() as block:
        ctx = Ctx(nc, block)
        with ctx.stack:
            @sect(ctx, "y")
            def _(sync):
                ctx.mark(sync.dma_start(out=xres[:, :], in_=x_in[:, :]), "dml", 16)

            blocks = {1: lambda: emit_l0_attn(ctx, xres, W, stage, substage, dbg),
                      2: lambda: emit_l0_ffn(ctx, xres, W),
                      3: lambda: emit_l1_attn(ctx, xres, W),
                      4: lambda: emit_l1_ffn(ctx, xres, W),
                      5: lambda: emit_l2_attn(ctx, xres, W),
                      6: lambda: emit_l2_ffn(ctx, xres, W, ccbufs, dbg=(dbg == 'l2f')),
                      7: lambda: emit_l3_attn(ctx, xres, W),
                      8: lambda: emit_l3_ffn(ctx, xres, W, ccbufs,
                                              dbg=(dbg == 'l3f'))}
            _reg_blocks(blocks, ctx, xres, W)
            for k in sorted(blocks):
                run = (k == only) if only is not None else (upto >= k)
                if run:
                    blocks[k]()
                    ctx.fence(["v", "s", "t", "g", "y"],
                              ("v", "s", "t", "g", "dml", "dms"))

            ctx.fence(["g"])
            @sect(ctx, "g")
            def _(g):
                ctx.mark(g.dma_start(out=y_out[:, :], in_=xres[:, :]), "dms", 16)
                g.wait_ge(ctx.sem["dms"], ctx.tot["dms"])
    return nc


def _reg_blocks(blocks, ctx, xres, W):
    pass


def emit_ln_stats(ctx, xbuf, st, scratch, rms=False):
    nc = ctx.nc

    @sect(ctx, "s")
    def _(sc):
        ctx.mark(sc.activation(scratch, xbuf, AF.Square, accum_out=st[:, 5:6]), "s")

    @sect(ctx, "v")
    def _(v):
        ctx.mark(v.reduce_sum(st[:, 4:5], xbuf, axis=AX.X), "v")
    ctx.fence(["v"], ["s"])

    @sect(ctx, "v")
    def _(v):
        v.tensor_scalar_mul(st[:, 0:1], st[:, 4:5], 1.0 / D)
        v.tensor_scalar_mul(st[:, 4:5], st[:, 5:6], 1.0 / D)
        v.drain()
        v.tensor_tensor(st[:, 5:6], st[:, 0:1], st[:, 0:1], ALU.mult)
        v.drain()
        v.tensor_sub(st[:, 4:5], st[:, 4:5], st[:, 5:6])
        v.drain()
        v.tensor_scalar_add(st[:, 5:6], st[:, 4:5], 1e-5)
        v.drain()
        inst = v.reciprocal(st[:, 5:6], st[:, 5:6])
        if rms:
            v.drain()
            v.tensor_tensor(st[:, 6:7], st[:, 4:5], st[:, 5:6], ALU.mult)
            v.drain()
            v.tensor_scalar_add(st[:, 6:7], st[:, 6:7], 1e-6)
            v.drain()
            inst = v.reciprocal(st[:, 6:7], st[:, 6:7])
        ctx.mark(inst, "v")
    ctx.fence(["s"], ["v"])

    @sect(ctx, "s")
    def _(sc):
        n = 2 if rms else 1
        ctx.mark(sc.activation(st[:, 5:5 + n], st[:, 5:5 + n], AF.Sqrt), "s")
    ctx.fence(["v"], ["s"])

    @sect(ctx, "v")
    def _(v):
        v.tensor_copy(st[:, 1:2], st[:, 5:6])
        v.drain()
        v.tensor_scalar(st[:, 2:3], st[:, 0:1], st[:, 1:2], -1.0, ALU.mult, ALU.mult)
        inst = v.tensor_scalar_add(st[:, 3:4], st[:, 1:2], 1.0)
        if rms:
            v.tensor_tensor(st[:, 7:8], st[:, 1:2], st[:, 6:7], ALU.mult)
            v.drain()
            inst = v.tensor_scalar(st[:, 5:6], st[:, 0:1], st[:, 7:8], -1.0, ALU.mult, ALU.mult)
        ctx.mark(inst, "v")


def emit_l0_attn(ctx, xres, W, stage=99, substage=99, dbg=None):
    nc = ctx.nc
    if dbg is not None:
        dbg_q = nc.dram_tensor("dbg_q", [128, DC * SEQ], BF16, kind="ExternalOutput").ap()
        dbg_k = nc.dram_tensor("dbg_k", [128, DC * SEQ], BF16, kind="ExternalOutput").ap()
        dbg_v = nc.dram_tensor("dbg_v", [128, NT_SEQ * D], BF16, kind="ExternalOutput").ap()
    with (
        nc.sbuf_tensor(ctx.nm("l0_ident"), [128, 128], BF16) as ident,
        nc.sbuf_tensor(ctx.nm("l0_bqkv"), [128, 16], F32) as bqkv,
        nc.sbuf_tensor(ctx.nm("l0_gn"), [128, DC], F32) as gnorm,
        nc.sbuf_tensor(ctx.nm("l0_bv"), [128, D], F32) as bv,
        nc.sbuf_tensor(ctx.nm("l0_bproj"), [128, D], F32) as bproj,
        nc.sbuf_tensor(ctx.nm("l0_stats"), [128, NT_SEQ * 8], F32) as stats,
        nc.sbuf_tensor(ctx.nm("l0_qf"), [128, DC * SEQ], BF16) as qf,
        nc.sbuf_tensor(ctx.nm("l0_kf"), [128, DC * SEQ], BF16) as kf,
        nc.sbuf_tensor(ctx.nm("l0_vt"), [128, NT_SEQ * D], BF16) as vt,
    ):
        @sect(ctx, "y")
        def _(sync):
            ctx.mark(sync.dma_start(out=ident[:], in_=W["ident"][:, :]), "dml", 16)
            ctx.mark(sync.dma_start(out=bqkv[:], in_=W["L0a_bqkv_pc"][:, :]), "dml", 16)
            ctx.mark(sync.dma_start(out=gnorm[:], in_=W["L0a_gnorm_pc"][:, :]), "dml", 16)
            ctx.mark(sync.dma_start(out=bv[:], in_=W["L0a_bv_bc"][:, :]), "dml", 16)
            ctx.mark(sync.dma_start(out=bproj[:], in_=W["L0a_bproj_bc"][:, :]), "dml", 16)

        def _dump(g):
            ctx.mark(g.dma_start(out=dbg_q[:, :], in_=qf[:, :]), "dms", 16)
            ctx.mark(g.dma_start(out=dbg_k[:, :], in_=kf[:, :]), "dms", 16)
            ctx.mark(g.dma_start(out=dbg_v[:, :], in_=vt[:, :]), "dms", 16)
            g.wait_ge(ctx.sem["dms"], ctx.tot["dms"])

        for s in range(NSEQ):
            if stage >= 1:
                _l0a_qkv_phase(ctx, s, xres, W, ident, bqkv, gnorm, bv, stats, qf, kf, vt, substage,
                               dbg if s == 0 else None)
            if s == 0 and dbg == "qkv":
                ctx.fence(["g"])
                sect(ctx, "g")(_dump)
            if stage >= 2:
                _l0a_rope_phase(ctx, s, W, qf, kf)
            if s == 0 and dbg == "rope":
                ctx.fence(["g"])
                sect(ctx, "g")(_dump)
            if stage >= 3:
                _l0a_attn_phase(ctx, s, xres, W, ident, bproj, stats, qf, kf, vt)


def _l0a_qkv_phase(ctx, s, xres, W, ident, bqkv, gnorm, bv, stats, qf, kf, vt, substage=99, dbg=None):
    nc = ctx.nc
    tok0 = s * SEQ
    if dbg == "lns":
        dbg_lns = nc.dram_tensor("dbg_lns", [128, 2 * DC * 512], BF16, kind="ExternalOutput").ap()
        dbg_st = nc.dram_tensor("dbg_st", [128, NT_SEQ * 8], F32, kind="ExternalOutput").ap()
    with (
        nc.sbuf_tensor(ctx.nm("l0_wg"), [128, DC * 3 * D], BF16) as wg,
        nc.sbuf_tensor(ctx.nm("l0_wstage"), [128, 3 * D], F32) as wstage,
        nc.sbuf_tensor(ctx.nm("l0_lns"), [128, 2 * DC * 512], BF16) as lns,
        nc.sbuf_tensor(ctx.nm("l0_xt"), [128, D], F32) as xt,
        nc.sbuf_tensor(ctx.nm("l0_sq"), [128, D], F32) as sq,
        nc.sbuf_tensor(ctx.nm("l0_lnb"), [128, D], BF16) as lnb,
        nc.psum_tensor(ctx.nm("l0_ptp"), [128, 1024], BF16) as ptp,
        nc.psum_tensor(ctx.nm("l0_mm0"), [128, 512], F32) as mm0,
        nc.psum_tensor(ctx.nm("l0_mm1"), [128, 512], F32) as mm1,
    ):
        for c in range(DC if substage >= 1 else 0):
            ctx.fence(["y"], ["v"])
            @sect(ctx, "y")
            def _(sync, c=c):
                ctx.mark(sync.dma_start(
                    out=wstage[:], in_=W["L0a_wqkv"][c * 128:(c + 1) * 128, :]), "dml", 16)
            ctx.fence(["v"], ["dml"])
            @sect(ctx, "v")
            def _(v, c=c):
                ctx.mark(v.tensor_scalar_mul(
                    wg[:, c * 3 * D:(c + 1) * 3 * D], wstage[:], gnorm[:, c:c + 1]), "v")

        for grp in range((NT_SEQ // 4) if substage >= 2 else 0):
            gbuf = lns[:, (grp % 2) * DC * 512:(grp % 2 + 1) * DC * 512]
            for tl in range(4):
                t = grp * 4 + tl
                st = stats[:, t * 8:(t + 1) * 8]
                ctx.fence(["y"], ["v", "s", "t"])
                @sect(ctx, "y")
                def _(sync, t=t):
                    ctx.mark(sync.dma_start(
                        out=xt[:], in_=xres[tok0 + t * 128: tok0 + (t + 1) * 128, :]), "dml", 16)
                ctx.fence(["v", "s"], ["dml"])
                emit_ln_stats(ctx, xt[:], st, sq[:], rms=True)
                ctx.fence(["s"], ["v"])
                @sect(ctx, "s")
                def _(sc, st=st):
                    ctx.mark(sc.activation(lnb[:], xt[:], AF.Identity,
                                           bias=st[:, 5:6], scale=st[:, 7:8]), "s")
                if substage < 3:
                    continue
                ctx.fence(["t"], ["s"])
                @sect(ctx, "t")
                def _(te):
                    for c in range(DC):
                        inst = te.transpose(ptp[:, (c % 8) * 128:(c % 8 + 1) * 128],
                                            lnb[:, c * 128:(c + 1) * 128], ident[:])
                    ctx.mark(inst, "t")
                ctx.fence(["s"], ["t"])
                @sect(ctx, "s")
                def _(sc, gbuf=gbuf, tl=tl):
                    for c in range(DC):
                        inst = sc.copy(gbuf[:, c * 512 + tl * 128: c * 512 + (tl + 1) * 128],
                                       ptp[:, (c % 8) * 128:(c % 8 + 1) * 128])
                    ctx.mark(inst, "s")

            if dbg == "lns" and grp == 1:
                ctx.fence(["g"])
                @sect(ctx, "g")
                def _(g):
                    ctx.mark(g.dma_start(out=dbg_lns[:, :], in_=lns[:, :]), "dms", 16)
                    ctx.mark(g.dma_start(out=dbg_st[:, :], in_=stats[:, :]), "dms", 16)
                ctx.fence(["s", "t", "v", "y", "g"], ["dms"])

            ctx.fence(["t"], ["s"])
            for oc in range(16 if substage >= 4 else 0):
                ps = mm0 if oc % 2 == 0 else mm1
                @sect(ctx, "t")
                def _(te, oc=oc, ps=ps, gbuf=gbuf):
                    for c in range(DC):
                        inst = te.matmul(ps[:], wg[:, c * 3 * D + oc * 128: c * 3 * D + (oc + 1) * 128],
                                         gbuf[:, c * 512:(c + 1) * 512],
                                         start=(c == 0), stop=(c == DC - 1))
                    ctx.mark(inst, "t")
                ctx.fence(["s"], ["t"])
                @sect(ctx, "s")
                def _(sc, oc=oc, ps=ps, grp=grp):
                    dst = (qf if oc < 8 else kf)[:, (oc % 8) * SEQ + grp * 512:
                                                 (oc % 8) * SEQ + (grp + 1) * 512]
                    ctx.mark(sc.activation(dst, ps[:], AF.Identity,
                                           bias=bqkv[:, oc:oc + 1]), "s")
                if oc < 15:
                    ctx.fence(["t"], ["s"])

            for tl in range(4 if substage >= 5 else 0):
                t = grp * 4 + tl
                ctx.fence(["t"], ["s", "v"])
                @sect(ctx, "t")
                def _(te, tl=tl, gbuf=gbuf):
                    for col in range(2):
                        ps = mm0 if col == 0 else mm1
                        for c in range(DC):
                            inst = te.matmul(ps[:],
                                             gbuf[:, c * 512 + tl * 128: c * 512 + (tl + 1) * 128],
                                             wg[:, c * 3 * D + 2 * D + col * 512:
                                                c * 3 * D + 2 * D + (col + 1) * 512],
                                             start=(c == 0), stop=(c == DC - 1))
                    ctx.mark(inst, "t")
                ctx.fence(["v"], ["t"])
                @sect(ctx, "v")
                def _(v, t=t):
                    v.tensor_add(vt[:, t * D: t * D + 512], mm0[:], bv[:, 0:512])
                    ctx.mark(v.tensor_add(vt[:, t * D + 512: (t + 1) * D], mm1[:],
                                          bv[:, 512:1024]), "v")


def _l0a_rope_phase(ctx, s, W, qf, kf):
    nc = ctx.nc
    with (
        nc.sbuf_tensor(ctx.nm("l0_ropec"), [128, 4 * SEQ], BF16) as ropec,
        nc.sbuf_tensor(ctx.nm("l0_ropes"), [128, 4 * SEQ], BF16) as ropes,
        nc.sbuf_tensor(ctx.nm("l0_sv"), [128, SEQ], BF16) as sv,
        nc.sbuf_tensor(ctx.nm("l0_rt1"), [128, SEQ], BF16) as rt1,
        nc.sbuf_tensor(ctx.nm("l0_rt2"), [128, SEQ], BF16) as rt2,
    ):
        ctx.fence(["y"], ["v", "s", "t"])
        @sect(ctx, "y")
        def _(sync):
            ctx.mark(sync.dma_start(out=ropec[:], in_=W["ropec"][:, :]), "dml", 16)
            ctx.mark(sync.dma_start(out=ropes[:], in_=W["ropes"][:, :]), "dml", 16)
        ctx.fence(["v"], ["dml", "s", "t"])
        for buf in (qf, kf):
            for hc in range(4):
                lo = buf[:, hc * SEQ:(hc + 1) * SEQ]
                hi = buf[:, (hc + 4) * SEQ:(hc + 5) * SEQ]
                cs = ropec[:, hc * SEQ:(hc + 1) * SEQ]
                sn = ropes[:, hc * SEQ:(hc + 1) * SEQ]
                @sect(ctx, "v")
                def _(v, lo=lo, hi=hi, cs=cs, sn=sn):
                    v.tensor_copy(sv[:], lo)
                    v.drain()
                    v.tensor_mul(rt1[:], sv[:], cs)
                    v.tensor_mul(rt2[:], hi, sn)
                    v.drain()
                    v.tensor_sub(lo, rt1[:], rt2[:])
                    v.drain()
                    v.tensor_mul(rt1[:], hi, cs)
                    v.tensor_mul(rt2[:], sv[:], sn)
                    v.drain()
                    ctx.mark(v.tensor_add(hi, rt1[:], rt2[:]), "v")


def _l0a_attn_phase(ctx, s, xres, W, ident, bproj, stats, qf, kf, vt):
    nc = ctx.nc
    tok0 = s * SEQ
    with (
        nc.sbuf_tensor(ctx.nm("l0_wproj"), [128, DC * D], BF16) as wproj,
        nc.sbuf_tensor(ctx.nm("l0_wps"), [128, D], F32) as wps,
        nc.sbuf_tensor(ctx.nm("l0_P"), [128, SEQ], BF16) as P,
        nc.sbuf_tensor(ctx.nm("l0_Pt"), [128, SEQ], BF16) as Pt,
        nc.sbuf_tensor(ctx.nm("l0_ob"), [128, D], BF16) as ob,
        nc.sbuf_tensor(ctx.nm("l0_ofm"), [128, D], BF16) as ofm,
        nc.sbuf_tensor(ctx.nm("l0_xt2"), [128, D], F32) as xt2,
        nc.sbuf_tensor(ctx.nm("l0_row"), [128, 4], F32) as row,
        nc.sbuf_tensor(ctx.nm("l0_eres"), [128, D], F32) as eres,
        nc.psum_tensor(ctx.nm("l0_S"), [128, SEQ], F32) as S,
        nc.psum_tensor(ctx.nm("l0_ops"), [128, D], F32) as ops,
        nc.psum_tensor(ctx.nm("l0_ptp2"), [128, 1024], BF16) as ptp,
    ):
        for c in range(DC):
            ctx.fence(["y"], ["v"])
            @sect(ctx, "y")
            def _(sync, c=c):
                ctx.mark(sync.dma_start(
                    out=wps[:], in_=W["L0a_wproj"][c * 128:(c + 1) * 128, :]), "dml", 16)
            ctx.fence(["v"], ["dml"])
            @sect(ctx, "v")
            def _(v, c=c):
                ctx.mark(v.tensor_copy(wproj[:, c * D:(c + 1) * D], wps[:]), "v")

        for qt in range(NT_SEQ):
            st = stats[:, qt * 8:(qt + 1) * 8]
            ctx.fence(["t"], ["v", "s"])
            @sect(ctx, "t")
            def _(te, qt=qt):
                for c in range(DC):
                    lhs = qf[:, c * SEQ + qt * 128: c * SEQ + (qt + 1) * 128]
                    for col in range(4):
                        inst = te.matmul(S[:, col * 512:(col + 1) * 512], lhs,
                                         kf[:, c * SEQ + col * 512: c * SEQ + (col + 1) * 512],
                                         start=(c == 0), stop=(c == DC - 1))
                ctx.mark(inst, "t")
            ctx.fence(["v"], ["t"])
            @sect(ctx, "v")
            def _(v):
                v.reduce_max(row[:, 0:1], S[:], axis=AX.X)
                v.drain()
                ctx.mark(v.tensor_scalar_mul(row[:, 1:2], row[:, 0:1], -LN3 / 32.0), "v")
            ctx.fence(["s"], ["v"])
            @sect(ctx, "s")
            def _(sc):
                ctx.mark(sc.activation(P[:], S[:], AF.Exp, bias=row[:, 1:2],
                                       scale=LN3 / 32.0, accum_out=row[:, 2:3]), "s")
            ctx.fence(["v"], ["s"])
            @sect(ctx, "v")
            def _(v):
                ctx.mark(v.reciprocal(row[:, 3:4], row[:, 2:3]), "v")
            for b in range(2):
                ctx.fence(["t"], ["s", "v"])
                @sect(ctx, "t")
                def _(te, b=b):
                    for k in range(8):
                        kc = b * 8 + k
                        inst = te.transpose(ptp[:, k * 128:(k + 1) * 128],
                                            P[:, kc * 128:(kc + 1) * 128], ident[:])
                    ctx.mark(inst, "t")
                ctx.fence(["s"], ["t"])
                @sect(ctx, "s")
                def _(sc, b=b):
                    for k in range(8):
                        kc = b * 8 + k
                        inst = sc.copy(Pt[:, kc * 128:(kc + 1) * 128], ptp[:, k * 128:(k + 1) * 128])
                    ctx.mark(inst, "s")
            ctx.fence(["t"], ["s"])
            @sect(ctx, "t")
            def _(te):
                for kc in range(NT_SEQ):
                    lhs = Pt[:, kc * 128:(kc + 1) * 128]
                    for col in range(2):
                        inst = te.matmul(ops[:, col * 512:(col + 1) * 512], lhs,
                                         vt[:, kc * D + col * 512: kc * D + (col + 1) * 512],
                                         start=(kc == 0), stop=(kc == NT_SEQ - 1))
                ctx.mark(inst, "t")
            ctx.fence(["s"], ["t", "v"])
            @sect(ctx, "s")
            def _(sc):
                ctx.mark(sc.activation(ob[:], ops[:], AF.Identity, scale=row[:, 3:4]), "s")
            ctx.fence(["t"], ["s"])
            @sect(ctx, "t")
            def _(te):
                for c in range(DC):
                    inst = te.transpose(ptp[:, c * 128:(c + 1) * 128],
                                        ob[:, c * 128:(c + 1) * 128], ident[:])
                ctx.mark(inst, "t")
            ctx.fence(["s"], ["t"])
            @sect(ctx, "s")
            def _(sc):
                for c in range(DC):
                    inst = sc.copy(ofm[:, c * 128:(c + 1) * 128], ptp[:, c * 128:(c + 1) * 128])
                ctx.mark(inst, "s")
            ctx.fence(["y"], ["s", "g"])
            @sect(ctx, "y")
            def _(sync, qt=qt):
                ctx.mark(sync.dma_start(
                    out=xt2[:], in_=xres[tok0 + qt * 128: tok0 + (qt + 1) * 128, :]), "dml", 16)
            ctx.fence(["t"], ["s"])
            @sect(ctx, "t")
            def _(te):
                for c in range(DC):
                    lhs = ofm[:, c * 128:(c + 1) * 128]
                    for col in range(2):
                        inst = te.matmul(ops[:, col * 512:(col + 1) * 512], lhs,
                                         wproj[:, c * D + col * 512: c * D + (col + 1) * 512],
                                         start=(c == 0), stop=(c == DC - 1))
                ctx.mark(inst, "t")
            ctx.fence(["s"], ["t", "dml", "dms"])
            @sect(ctx, "s")
            def _(sc, st=st):
                ctx.mark(sc.activation(eres[:], xt2[:], AF.Identity,
                                       bias=st[:, 2:3], scale=st[:, 3:4]), "s")
            ctx.fence(["v"], ["s"])
            @sect(ctx, "v")
            def _(v):
                v.tensor_add(eres[:], eres[:], bproj[:])
                v.drain()
                ctx.mark(v.tensor_add(eres[:], eres[:], ops[:]), "v")
            ctx.fence(["g"], ["v"])
            @sect(ctx, "g")
            def _(g, qt=qt):
                ctx.mark(g.dma_start(
                    out=xres[tok0 + qt * 128: tok0 + (qt + 1) * 128, :], in_=eres[:]), "dms", 16)


def make_in_maps(inputs):
    x = np.ascontiguousarray(inputs["L0_attn_x"], dtype=np.float32)
    ident = np.eye(128, dtype=ml_dtypes.bfloat16)
    half = D // 2
    inv_freq = 1.0 / (10000.0 ** (np.arange(half, dtype=np.float32) / half))
    ang = np.arange(SEQ, dtype=np.float32)[:, None] * inv_freq[None, :]
    def _fm(tab):
        t = tab.T.reshape(4, 128, SEQ).transpose(1, 0, 2).reshape(128, 4 * SEQ)
        return np.ascontiguousarray(t.astype(ml_dtypes.bfloat16))
    ropec = _fm(np.round(np.cos(ang)))
    ropes = _fm(np.round(np.sin(ang)))

    bqkv = np.asarray(inputs["L0_attn_b_qkv"], np.float32)
    shared = {
        "ident": ident,
        "ropec": ropec,
        "ropes": ropes,
        "L0a_wqkv": np.ascontiguousarray(inputs["L0_attn_w_qkv"], np.float32),
        "L0a_bqkv_pc": np.ascontiguousarray(bqkv[:2 * D].reshape(16, 128).T),
        "L0a_bv_bc": np.ascontiguousarray(np.broadcast_to(bqkv[2 * D:], (128, D))),
        "L0a_gnorm_pc": np.ascontiguousarray(
            np.asarray(inputs["L0_attn_g_norm"], np.float32).reshape(DC, 128).T),
        "L0a_wproj": np.ascontiguousarray(inputs["L0_attn_w_proj"], np.float32),
        "L0a_bproj_bc": np.ascontiguousarray(
            np.broadcast_to(np.asarray(inputs["L0_attn_b_proj"], np.float32), (128, D))),
        "L0f_wq": np.ascontiguousarray(inputs["L0_ffn_Wq"], np.float32),
        "L0f_wk": np.ascontiguousarray(inputs["L0_ffn_Wk"], np.float32),
        "L0f_wsd": np.ascontiguousarray(np.concatenate(
            [np.asarray(inputs["L0_ffn_Ws"], np.float32).reshape(D, 8, 5),
             np.asarray(inputs["L0_ffn_Wd"], np.float32).reshape(D, 8, 5)],
            axis=1).reshape(D, 80)),
        "L0f_wv": np.ascontiguousarray(
            np.asarray(inputs["L0_ffn_Wv"], np.float32).transpose(1, 0, 2).reshape(D, 320)),
        "L0f_wo": np.concatenate(
            [np.asarray(inputs["L0_ffn_Wo"], np.float32).reshape(320, D),
             np.zeros((64, D), np.float32)], axis=0),
        "cmask": np.where(np.arange(128)[:, None] >= np.arange(128)[None, :],
                          np.float32(0.0), np.float32(-1e30)),
        "L1a_wq": np.ascontiguousarray(np.asarray(inputs["L1_attn_Wq"], np.float32).T),
        "L1a_wk": np.ascontiguousarray(np.asarray(inputs["L1_attn_Wk"], np.float32).T),
        "L1a_wv": np.ascontiguousarray(np.asarray(inputs["L1_attn_Wv"], np.float32).T),
        "L1a_wo": np.ascontiguousarray(np.asarray(inputs["L1_attn_Wo"], np.float32).T),
        "L1f_gate": np.ascontiguousarray(inputs["L1_ffn_gate_w"], np.float32),
        "L1f_w1": np.ascontiguousarray(inputs["L1_ffn_w1"], np.float32),
        "L1f_w2": np.ascontiguousarray(inputs["L1_ffn_w2"], np.float32),
        "L1f_w3": np.ascontiguousarray(inputs["L1_ffn_w3"], np.float32),
        "L2a_win": np.ascontiguousarray(np.asarray(inputs["L2_attn_input_weights"], np.float32).T),
        "L2a_wout": np.ascontiguousarray(np.asarray(inputs["L2_attn_output_weight"], np.float32).T),
        "L2f_wup": np.ascontiguousarray(inputs["L2_ffn_w_up"], np.float32),
        "L2f_wgate": np.ascontiguousarray(inputs["L2_ffn_w_gate"], np.float32),
        "L2f_wdown": np.ascontiguousarray(inputs["L2_ffn_w_down"], np.float32),
        "L3a_wv_p": np.ascontiguousarray(
            np.asarray(inputs["L3_attn_W_qkv"], np.float32)[:, 2 * D:3 * D]),
        "L3a_wqk_t": np.ascontiguousarray(
            np.asarray(inputs["L3_attn_W_qkv"], np.float32)[:, 0:2 * D]
            .reshape(8, 128, 16, 128).transpose(2, 1, 0, 3).reshape(16, 128, D)),
        "L3a_wf": np.ascontiguousarray(np.vstack(
            [np.asarray(inputs["L3_attn_W_f"], np.float32)] * 2)),
        "L3a_bf_pc": np.ascontiguousarray(np.vstack(
            [np.asarray(inputs["L3_attn_b_f"], np.float32).reshape(64, 1)] * 2)),
        "L3a_wout": np.ascontiguousarray(inputs["L3_attn_W_out"], np.float32),
        "L3a_bout_bc": np.ascontiguousarray(np.broadcast_to(
            np.asarray(inputs["L3_attn_b_out"], np.float32), (128, D))),
        "L3f_wq": np.ascontiguousarray(inputs["L3_ffn_Wq"], np.float32),
        "L3f_wk": np.ascontiguousarray(inputs["L3_ffn_Wk"], np.float32),
        "L3f_wv": np.ascontiguousarray(inputs["L3_ffn_Wv"], np.float32),
        "L3f_wo": np.ascontiguousarray(inputs["L3_ffn_Wo"], np.float32),
        "L3f_bq_pc": np.ascontiguousarray(
            np.asarray(inputs["L3_ffn_bq"], np.float32).reshape(8, 128).T),
        "L3f_bk_pc": np.ascontiguousarray(
            np.asarray(inputs["L3_ffn_bk"], np.float32).reshape(8, 128).T),
        "L3f_bv_bc": np.ascontiguousarray(np.broadcast_to(
            np.asarray(inputs["L3_ffn_bv"], np.float32), (128, D))),
        "L3f_bo_bc": np.ascontiguousarray(np.broadcast_to(
            np.asarray(inputs["L3_ffn_bo"], np.float32), (128, D))),
        "L3f_ln1g_bc": np.ascontiguousarray(np.broadcast_to(
            np.asarray(inputs["L3_ffn_ln1_g"], np.float32), (128, D))),
        "L3f_ln1b_bc": np.ascontiguousarray(np.broadcast_to(
            np.asarray(inputs["L3_ffn_ln1_b"], np.float32), (128, D))),
        "L3f_ln2g_bc": np.ascontiguousarray(np.broadcast_to(
            np.asarray(inputs["L3_ffn_ln2_g"], np.float32), (128, D))),
        "L3f_ln2b_bc": np.ascontiguousarray(np.broadcast_to(
            np.asarray(inputs["L3_ffn_ln2_b"], np.float32), (128, D))),
        "L3f_gate": np.ascontiguousarray(inputs["L3_ffn_gate_w"], np.float32),
        "L3f_gateb_bc": np.ascontiguousarray(np.broadcast_to(
            np.asarray(inputs["L3_ffn_gate_b"], np.float32), (128, 8))),
        "L3f_w1t": np.ascontiguousarray(
            np.asarray(inputs["L3_ffn_W1"], np.float32)
            .reshape(8, 8, 128, 32, 128).transpose(0, 3, 2, 1, 4).reshape(8, 32, 128, D)),
        "L3f_b1_pc": np.ascontiguousarray(
            np.asarray(inputs["L3_ffn_b1"], np.float32).reshape(8, 32, 128)
            .transpose(2, 0, 1).reshape(128, 256)),
        "L3f_w2": np.ascontiguousarray(inputs["L3_ffn_W2"], np.float32),
        "L3f_b2_bc": np.ascontiguousarray(np.broadcast_to(
            np.asarray(inputs["L3_ffn_b2"], np.float32).reshape(1, 8 * D), (128, 8 * D))),
    }
    return [dict(shared, x=x[2 * c:2 * c + 2].reshape(NTOK, D)) for c in range(8)]


def kernel(**inputs):
    nc = build_program()
    in_maps = make_in_maps(inputs)
    res = run_bass_kernel_spmd(nc, in_maps, core_ids=list(range(8)))
    out = np.stack([res.results[c]["y"] for c in range(8)])
    return out.reshape(16, SEQ, D)


def emit_l0_ffn(ctx, xres, W):
    nc = ctx.nc
    with (
        nc.sbuf_tensor(ctx.nm("f0_ident"), [128, 128], BF16) as ident,
        nc.sbuf_tensor(ctx.nm("f0_wq"), [128, DC * 512], BF16) as wq,
        nc.sbuf_tensor(ctx.nm("f0_wk"), [128, DC * 512], BF16) as wk,
        nc.sbuf_tensor(ctx.nm("f0_wsd"), [128, DC * 80], BF16) as wsd,
        nc.sbuf_tensor(ctx.nm("f0_wv"), [128, DC * 320], BF16) as wv_w,
        nc.sbuf_tensor(ctx.nm("f0_wo"), [128, 3 * D], BF16) as wo,
        nc.sbuf_tensor(ctx.nm("f0_wstage"), [128, D], F32) as wstage,
        nc.sbuf_tensor(ctx.nm("f0_stats"), [128, NT_SEQ * 8], F32) as stats,
    ):
        @sect(ctx, "y")
        def _(sync):
            ctx.mark(sync.dma_start(out=ident[:], in_=W["ident"][:, :]), "dml", 16)
        specs = [(W["L0f_wq"], wq, 512), (W["L0f_wk"], wk, 512),
                 (W["L0f_wsd"], wsd, 80), (W["L0f_wv"], wv_w, 320)]
        for src_w, dst, width in specs:
            for c in range(DC):
                ctx.fence(["y"], ["v"])
                @sect(ctx, "y")
                def _(sync, src_w=src_w, c=c, width=width):
                    ctx.mark(sync.dma_start(
                        out=wstage[:, 0:width], in_=src_w[c * 128:(c + 1) * 128, :]), "dml", 16)
                ctx.fence(["v"], ["dml"])
                @sect(ctx, "v")
                def _(v, dst=dst, c=c, width=width):
                    ctx.mark(v.tensor_copy(dst[:, c * width:(c + 1) * width],
                                           wstage[:, 0:width]), "v")
        for c in range(3):
            ctx.fence(["y"], ["v"])
            @sect(ctx, "y")
            def _(sync, c=c):
                ctx.mark(sync.dma_start(
                    out=wstage[:, 0:D], in_=W["L0f_wo"][c * 128:(c + 1) * 128, :]), "dml", 16)
            ctx.fence(["v"], ["dml"])
            @sect(ctx, "v")
            def _(v, c=c):
                ctx.mark(v.tensor_copy(wo[:, c * D:(c + 1) * D], wstage[:, 0:D]), "v")

        for s in range(NSEQ):
            _l0f_one_seq(ctx, s, xres, W, ident, wq, wk, wsd, wv_w, wo, stats)


def _l0f_one_seq(ctx, s, xres, W, ident, wq, wk, wsd, wv_w, wo, stats):
    nc = ctx.nc
    tok0 = s * SEQ
    es = contextlib.ExitStack()
    with es:
        lnf = es.enter_context(nc.sbuf_tensor(ctx.nm("f0_lnf"), [128, DC * SEQ], BF16))
        qh = es.enter_context(nc.sbuf_tensor(ctx.nm("f0_qh"), [128, 4 * SEQ], BF16))
        kh = es.enter_context(nc.sbuf_tensor(ctx.nm("f0_kh"), [128, 4 * SEQ], BF16))
        vt = es.enter_context(nc.sbuf_tensor(ctx.nm("f0_vt"), [128, NT_SEQ * 512], BF16))
        mo_all = es.enter_context(nc.sbuf_tensor(ctx.nm("f0_mo"), [128, NT_SEQ * 40], F32))
        xt = es.enter_context(nc.sbuf_tensor(ctx.nm("f0_xt"), [128, D], F32))
        sq = es.enter_context(nc.sbuf_tensor(ctx.nm("f0_sq"), [128, D], F32))
        lnb = es.enter_context(nc.sbuf_tensor(ctx.nm("f0_lnb"), [128, D], BF16))
        gl = es.enter_context(nc.sbuf_tensor(ctx.nm("f0_gl"), [128, 80], F32))
        mk = es.enter_context(nc.sbuf_tensor(ctx.nm("f0_mk"), [128, 96], F32))
        xv = es.enter_context(nc.sbuf_tensor(ctx.nm("f0_xv"), [128, 320], F32))
        vmx = es.enter_context(nc.sbuf_tensor(ctx.nm("f0_vmx"), [128, 2 * 512], F32))
        P = es.enter_context(nc.sbuf_tensor(ctx.nm("f0_P"), [128, SEQ], BF16))
        Pt = es.enter_context(nc.sbuf_tensor(ctx.nm("f0_Pt"), [128, SEQ], BF16))
        attn = es.enter_context(nc.sbuf_tensor(ctx.nm("f0_at"), [128, 512], BF16))
        row = es.enter_context(nc.sbuf_tensor(ctx.nm("f0_row"), [128, 4], F32))
        Mt = es.enter_context(nc.sbuf_tensor(ctx.nm("f0_Mt"), [128, 384], BF16))
        Mf = es.enter_context(nc.sbuf_tensor(ctx.nm("f0_Mf"), [128, 384], BF16))
        eres = es.enter_context(nc.sbuf_tensor(ctx.nm("f0_er"), [128, D], F32))
        S = es.enter_context(nc.psum_tensor(ctx.nm("f0_S"), [128, SEQ], F32))
        mm0 = es.enter_context(nc.psum_tensor(ctx.nm("f0_mm0"), [128, 512], F32))
        ptp = es.enter_context(nc.psum_tensor(ctx.nm("f0_ptp"), [128, 1024], BF16))
        for t in range(NT_SEQ):
            st = stats[:, t * 8:(t + 1) * 8]
            ctx.fence(["y"], ["v", "s", "t"])
            @sect(ctx, "y")
            def _(sync, t=t):
                ctx.mark(sync.dma_start(
                    out=xt[:], in_=xres[tok0 + t * 128: tok0 + (t + 1) * 128, :]), "dml", 16)
            ctx.fence(["v", "s"], ["dml"])
            emit_ln_stats(ctx, xt[:], st, sq[:], rms=False)
            ctx.fence(["s"], ["v"])
            @sect(ctx, "s")
            def _(sc, st=st):
                ctx.mark(sc.activation(lnb[:], xt[:], AF.Identity,
                                       bias=st[:, 2:3], scale=st[:, 1:2]), "s")
            ctx.fence(["t"], ["s"])
            @sect(ctx, "t")
            def _(te, t=t):
                for c in range(DC):
                    inst = te.transpose(ptp[:, c * 128:(c + 1) * 128],
                                        lnb[:, c * 128:(c + 1) * 128], ident[:])
                ctx.mark(inst, "t")
            ctx.fence(["s"], ["t"])
            @sect(ctx, "s")
            def _(sc, t=t):
                for c in range(DC):
                    inst = sc.copy(lnf[:, c * SEQ + t * 128: c * SEQ + (t + 1) * 128],
                                   ptp[:, c * 128:(c + 1) * 128])
                ctx.mark(inst, "s")
            ctx.fence(["t"], ["s"])
            @sect(ctx, "t")
            def _(te, t=t):
                for c in range(DC):
                    inst = te.matmul(mm0[:, 0:80],
                                     lnf[:, c * SEQ + t * 128: c * SEQ + (t + 1) * 128],
                                     wsd[:, c * 80:(c + 1) * 80],
                                     start=(c == 0), stop=(c == DC - 1))
                ctx.mark(inst, "t")
            ctx.fence(["s"], ["t"])
            @sect(ctx, "s")
            def _(sc):
                inst = sc.copy(gl[:], mm0[:, 0:80])
                ctx.mark(inst, "s")
            ctx.fence(["v"], ["s"])
            for half, base in ((0, 0), (1, 40)):
                @sect(ctx, "v")
                def _(v, base=base, half=half, t=t):
                    mcol = mk[:, 48 * half: 48 * half + 8]
                    work = mk[:, 48 * half + 8: 48 * half + 48]
                    g5 = gl[:, base:base + 40].rearrange("p (h e) -> p h e", e=5)
                    v.tensor_reduce(mcol, g5, axis=AX.X, op=ALU.max)
                    v.drain()
                    m1b = mcol.rearrange("p (h e) -> p h e", e=1).broadcast_to([128, 8, 5])
                    v.tensor_tensor(work.rearrange("p (h e) -> p h e", e=5),
                                    g5, m1b, ALU.is_equal)
                    v.drain()
                    v.scalar_tensor_tensor(work, work, -1e30,
                                           gl[:, base:base + 40], ALU.mult, ALU.add)
                    v.drain()
                    v.tensor_reduce(mcol, work.rearrange("p (h e) -> p h e", e=5),
                                    axis=AX.X, op=ALU.max)
                    v.drain()
                    dst = work if half == 0 else mo_all[:, t * 40:(t + 1) * 40]
                    inst = v.tensor_tensor(dst.rearrange("p (h e) -> p h e", e=5),
                                           g5, m1b, ALU.is_ge)
                    ctx.mark(inst, "v")
            ctx.fence(["s"], ["v"])
            @sect(ctx, "s")
            def _(sc):
                ctx.mark(sc.activation(gl[:, 0:40], gl[:, 0:40], AF.Sigmoid), "s")
            ctx.fence(["v"], ["s"])
            @sect(ctx, "v")
            def _(v):
                ctx.mark(v.tensor_mul(mk[:, 8:48], mk[:, 8:48], gl[:, 0:40]), "v")
            ctx.fence(["t"], ["s", "v"])
            @sect(ctx, "t")
            def _(te, t=t):
                for c in range(DC):
                    inst = te.matmul(mm0[:, 0:320],
                                     lnf[:, c * SEQ + t * 128: c * SEQ + (t + 1) * 128],
                                     wv_w[:, c * 320:(c + 1) * 320],
                                     start=(c == 0), stop=(c == DC - 1))
                ctx.mark(inst, "t")
            ctx.fence(["s"], ["t"])
            @sect(ctx, "s")
            def _(sc):
                ctx.mark(sc.copy(xv[:], mm0[:, 0:320]), "s")
            ctx.fence(["v"], ["s"])
            @sect(ctx, "v")
            def _(v, t=t):
                wv5 = mk[:, 8:48].rearrange("p (h e) -> p h e", e=5)
                for e in range(5):
                    wvb = wv5[:, :, e:e + 1].broadcast_to([128, 8, 64])
                    xvb = xv[:, e * 64:(e + 1) * 64].rearrange(
                        "p (o k) -> p o k", o=1).broadcast_to([128, 8, 64])
                    dst = vmx[:, (e % 2) * 512:(e % 2) * 512 + 512]
                    d3 = dst.rearrange("p (h k) -> p h k", k=64)
                    if e == 0:
                        v.tensor_tensor(d3, wvb, xvb, ALU.mult)
                    elif e == 1:
                        v.tensor_tensor(d3, wvb, xvb, ALU.mult)
                    else:
                        v.drain()
                        v.tensor_tensor(sq[:, 0:512].rearrange("p (h k) -> p h k", k=64),
                                        wvb, xvb, ALU.mult)
                        v.drain()
                        v.tensor_add(vmx[:, 0:512], vmx[:, 0:512], sq[:, 0:512])
                v.drain()
                inst = v.tensor_add(vt[:, t * 512:(t + 1) * 512],
                                    vmx[:, 0:512], vmx[:, 512:1024])
                ctx.mark(inst, "v")

        ctx.fence(["t"], ["s", "v"])
        for which, wmat, dstbuf in (("q", wq, qh), ("k", wk, kh)):
            for oc in range(4):
                @sect(ctx, "t")
                def _(te, wmat=wmat, oc=oc):
                    for c in range(DC):
                        for col in range(4):
                            inst = te.matmul(
                                S[:, col * 512:(col + 1) * 512],
                                wmat[:, c * 512 + oc * 128: c * 512 + (oc + 1) * 128],
                                lnf[:, c * SEQ + col * 512: c * SEQ + (col + 1) * 512],
                                start=(c == 0), stop=(c == DC - 1))
                    ctx.mark(inst, "t")
                ctx.fence(["s"], ["t"])
                @sect(ctx, "s")
                def _(sc, which=which, dstbuf=dstbuf, oc=oc):
                    scl = 0.125 if which == "q" else 1.0
                    inst = sc.activation(dstbuf[:, oc * SEQ:(oc + 1) * SEQ],
                                         S[:, :], AF.Identity, scale=scl)
                    ctx.mark(inst, "s")
                ctx.fence(["t"], ["s"])

        for qt in range(NT_SEQ):
            for h in range(8):
                oc, rb = h // 2, 64 * (h % 2)
                ctx.fence(["t"], ["v", "s"])
                @sect(ctx, "t")
                def _(te, oc=oc, rb=rb, qt=qt):
                    lhs = qh[rb:rb + 64, oc * SEQ + qt * 128: oc * SEQ + (qt + 1) * 128]
                    for col in range(4):
                        inst = te.matmul(
                            S[:, col * 512:(col + 1) * 512], lhs,
                            kh[rb:rb + 64, oc * SEQ + col * 512: oc * SEQ + (col + 1) * 512],
                            start=True, stop=True)
                    ctx.mark(inst, "t")
                ctx.fence(["v"], ["t"])
                @sect(ctx, "v")
                def _(v):
                    v.reduce_max(row[:, 0:1], S[:], axis=AX.X)
                    v.drain()
                    ctx.mark(v.tensor_scalar_mul(row[:, 1:2], row[:, 0:1], -1.0), "v")
                ctx.fence(["s"], ["v"])
                @sect(ctx, "s")
                def _(sc):
                    ctx.mark(sc.activation(P[:], S[:], AF.Exp, bias=row[:, 1:2],
                                           accum_out=row[:, 2:3]), "s")
                ctx.fence(["v"], ["s"])
                @sect(ctx, "v")
                def _(v):
                    ctx.mark(v.reciprocal(row[:, 3:4], row[:, 2:3]), "v")
                for b in range(2):
                    ctx.fence(["t"], ["s", "v"])
                    @sect(ctx, "t")
                    def _(te, b=b):
                        for k in range(8):
                            kc = b * 8 + k
                            inst = te.transpose(ptp[:, k * 128:(k + 1) * 128],
                                                P[:, kc * 128:(kc + 1) * 128], ident[:])
                        ctx.mark(inst, "t")
                    ctx.fence(["s"], ["t"])
                    @sect(ctx, "s")
                    def _(sc, b=b):
                        for k in range(8):
                            kc = b * 8 + k
                            inst = sc.copy(Pt[:, kc * 128:(kc + 1) * 128],
                                           ptp[:, k * 128:(k + 1) * 128])
                        ctx.mark(inst, "s")
                ctx.fence(["t"], ["s"])
                @sect(ctx, "t")
                def _(te, h=h):
                    for kc in range(NT_SEQ):
                        inst = te.matmul(mm0[:, 0:64], Pt[:, kc * 128:(kc + 1) * 128],
                                         vt[:, kc * 512 + h * 64: kc * 512 + (h + 1) * 64],
                                         start=(kc == 0), stop=(kc == NT_SEQ - 1))
                    ctx.mark(inst, "t")
                ctx.fence(["s"], ["t", "v"])
                @sect(ctx, "s")
                def _(sc, h=h):
                    ctx.mark(sc.activation(attn[:, h * 64:(h + 1) * 64], mm0[:, 0:64],
                                           AF.Identity, scale=row[:, 3:4]), "s")
                ctx.fence(["t"], ["s"])

            ctx.fence(["v"], ["s"])
            @sect(ctx, "v")
            def _(v, qt=qt):
                mo5 = mo_all[:, qt * 40:(qt + 1) * 40].rearrange("p (h e) -> p h e", e=5)
                for h in range(8):
                    mob = mo5[:, h:h + 1, :].rearrange(
                        "p o e -> p e o").broadcast_to([128, 5, 64])
                    atb = attn[:, h * 64:(h + 1) * 64].rearrange(
                        "p (o k) -> p o k", o=1).broadcast_to([128, 5, 64])
                    if h == 0:
                        v.tensor_tensor(vmx[:, 0:320].rearrange("p (e k) -> p e k", k=64),
                                        mob, atb, ALU.mult)
                    elif h == 1:
                        v.tensor_tensor(vmx[:, 512:832].rearrange("p (e k) -> p e k", k=64),
                                        mob, atb, ALU.mult)
                    else:
                        v.drain()
                        v.tensor_tensor(sq[:, 0:320].rearrange("p (e k) -> p e k", k=64),
                                        mob, atb, ALU.mult)
                        v.drain()
                        v.tensor_add(vmx[:, 0:320], vmx[:, 0:320], sq[:, 0:320])
                v.drain()
                v.tensor_add(Mt[:, 0:320], vmx[:, 0:320], vmx[:, 512:832])
                v.drain()
                inst = v.memset(Mt[:, 320:384], 0.0)
                ctx.mark(inst, "v")
            ctx.fence(["t"], ["v"])
            @sect(ctx, "t")
            def _(te):
                for c in range(3):
                    inst = te.transpose(ptp[:, c * 128:(c + 1) * 128],
                                        Mt[:, c * 128:(c + 1) * 128], ident[:])
                ctx.mark(inst, "t")
            ctx.fence(["s"], ["t"])
            @sect(ctx, "s")
            def _(sc):
                for c in range(3):
                    inst = sc.copy(Mf[:, c * 128:(c + 1) * 128], ptp[:, c * 128:(c + 1) * 128])
                ctx.mark(inst, "s")
            ctx.fence(["y"], ["s", "g"])
            @sect(ctx, "y")
            def _(sync, qt=qt):
                ctx.mark(sync.dma_start(
                    out=xt[:], in_=xres[tok0 + qt * 128: tok0 + (qt + 1) * 128, :]), "dml", 16)
            ctx.fence(["t"], ["s", "v"])
            @sect(ctx, "t")
            def _(te):
                for c in range(3):
                    for col in range(2):
                        inst = te.matmul(S[:, col * 512:(col + 1) * 512],
                                         Mf[:, c * 128:(c + 1) * 128],
                                         wo[:, c * D + col * 512: c * D + (col + 1) * 512],
                                         start=(c == 0), stop=(c == 2))
                ctx.mark(inst, "t")
            ctx.fence(["v"], ["t", "dml", "dms"])
            @sect(ctx, "v")
            def _(v):
                ctx.mark(v.tensor_add(eres[:], xt[:], S[:, 0:1024]), "v")
            ctx.fence(["g"], ["v"])
            @sect(ctx, "g")
            def _(g, qt=qt):
                ctx.mark(g.dma_start(
                    out=xres[tok0 + qt * 128: tok0 + (qt + 1) * 128, :], in_=eres[:]), "dms", 16)


def load_weight_bf16(ctx, srcw, dst, wstage, nchunks, width):
    nc = ctx.nc
    for c in range(nchunks):
        ctx.fence(["y"], ["v", "s", "t"])
        @sect(ctx, "y")
        def _(sync, c=c):
            ctx.mark(sync.dma_start(
                out=wstage[:, 0:width], in_=srcw[c * 128:(c + 1) * 128, :]), "dml", 16)
        ctx.fence(["v"], ["dml"])
        @sect(ctx, "v")
        def _(v, c=c):
            ctx.mark(v.tensor_copy(dst[:, c * width:(c + 1) * width],
                                   wstage[:, 0:width]), "v")


def emit_l1_attn(ctx, xres, W):
    nc = ctx.nc
    es = contextlib.ExitStack()
    with es:
        ident = es.enter_context(nc.sbuf_tensor(ctx.nm("a1_id"), [128, 128], BF16))
        cmask = es.enter_context(nc.sbuf_tensor(ctx.nm("a1_cm"), [128, 128], F32))
        wstage = es.enter_context(nc.sbuf_tensor(ctx.nm("a1_ws"), [128, D], F32))
        stats = es.enter_context(nc.sbuf_tensor(ctx.nm("a1_st"), [128, NT_SEQ * 8], F32))

        @sect(ctx, "y")
        def _(sync):
            ctx.mark(sync.dma_start(out=ident[:], in_=W["ident"][:, :]), "dml", 16)
            ctx.mark(sync.dma_start(out=cmask[:], in_=W["cmask"][:, :]), "dml", 16)

        for s in range(NSEQ):
            _l1a_one_seq(ctx, s, xres, W, ident, cmask, wstage, stats)


def _l1a_one_seq(ctx, s, xres, W, ident, cmask, wstage, stats):
    nc = ctx.nc
    tok0 = s * SEQ
    es = contextlib.ExitStack()
    with es:
        lnf = es.enter_context(nc.sbuf_tensor(ctx.nm("a1_lnf"), [128, DC * SEQ], BF16))
        qf = es.enter_context(nc.sbuf_tensor(ctx.nm("a1_qf"), [128, DC * SEQ], BF16))
        kf = es.enter_context(nc.sbuf_tensor(ctx.nm("a1_kf"), [128, DC * SEQ], BF16))
        vt = es.enter_context(nc.sbuf_tensor(ctx.nm("a1_vt"), [128, NT_SEQ * D], BF16))
        xt = es.enter_context(nc.sbuf_tensor(ctx.nm("a1_xt"), [128, D], F32))
        sq = es.enter_context(nc.sbuf_tensor(ctx.nm("a1_sq"), [128, D], F32))
        lnb = es.enter_context(nc.sbuf_tensor(ctx.nm("a1_lnb"), [128, D], BF16))
        P = es.enter_context(nc.sbuf_tensor(ctx.nm("a1_P"), [128, SEQ], BF16))
        Pt = es.enter_context(nc.sbuf_tensor(ctx.nm("a1_Pt"), [128, SEQ], BF16))
        attn = es.enter_context(nc.sbuf_tensor(ctx.nm("a1_at"), [128, D], BF16))
        atf = es.enter_context(nc.sbuf_tensor(ctx.nm("a1_atf"), [128, D], BF16))
        row = es.enter_context(nc.sbuf_tensor(ctx.nm("a1_row"), [128, 4], F32))
        eres = es.enter_context(nc.sbuf_tensor(ctx.nm("a1_er"), [128, D], F32))
        S = es.enter_context(nc.psum_tensor(ctx.nm("a1_S"), [128, SEQ], F32))
        mm = es.enter_context(nc.psum_tensor(ctx.nm("a1_mm"), [128, D], F32))
        ptp = es.enter_context(nc.psum_tensor(ctx.nm("a1_ptp"), [128, 1024], BF16))

        wvb = es.enter_context(nc.sbuf_tensor(ctx.nm("a1_wvb"), [128, DC * D], BF16))
        load_weight_bf16(ctx, W["L1a_wv"], wvb, wstage, DC, D)
        wv = wvb
        for t in range(NT_SEQ):
            st = stats[:, t * 8:(t + 1) * 8]
            ctx.fence(["y"], ["v", "s", "t"])
            @sect(ctx, "y")
            def _(sync, t=t):
                ctx.mark(sync.dma_start(
                    out=xt[:], in_=xres[tok0 + t * 128: tok0 + (t + 1) * 128, :]), "dml", 16)
            ctx.fence(["v", "s"], ["dml"])
            emit_ln_stats(ctx, xt[:], st, sq[:], rms=False)
            ctx.fence(["s"], ["v"])
            @sect(ctx, "s")
            def _(sc, st=st):
                ctx.mark(sc.activation(lnb[:], xt[:], AF.Identity,
                                       bias=st[:, 2:3], scale=st[:, 1:2]), "s")
            ctx.fence(["t"], ["s"])
            @sect(ctx, "t")
            def _(te, t=t):
                for c in range(DC):
                    inst = te.transpose(ptp[:, c * 128:(c + 1) * 128],
                                        lnb[:, c * 128:(c + 1) * 128], ident[:])
                ctx.mark(inst, "t")
            ctx.fence(["s"], ["t"])
            @sect(ctx, "s")
            def _(sc, t=t):
                for c in range(DC):
                    inst = sc.copy(lnf[:, c * SEQ + t * 128: c * SEQ + (t + 1) * 128],
                                   ptp[:, c * 128:(c + 1) * 128])
                ctx.mark(inst, "s")
            ctx.fence(["t"], ["s", "v"])
            @sect(ctx, "t")
            def _(te, t=t):
                for col in range(2):
                    for c in range(DC):
                        inst = te.matmul(
                            mm[:, col * 512:(col + 1) * 512],
                            lnf[:, c * SEQ + t * 128: c * SEQ + (t + 1) * 128],
                            wv[:, c * D + col * 512: c * D + (col + 1) * 512],
                            start=(c == 0), stop=(c == DC - 1))
                ctx.mark(inst, "t")
            ctx.fence(["v"], ["t"])
            @sect(ctx, "v")
            def _(v, t=t):
                ctx.mark(v.tensor_copy(vt[:, t * D:(t + 1) * D], mm[:, :]), "v")

        es_b = contextlib.ExitStack()
        wqk = es_b.enter_context(nc.sbuf_tensor(ctx.nm("a1_wqk"), [128, 2 * DC * D], BF16))
        load_weight_bf16(ctx, W["L1a_wq"], wqk, wstage, DC, D)
        for c in range(DC):
            ctx.fence(["y"], ["v"])
            @sect(ctx, "y")
            def _(sync, c=c):
                ctx.mark(sync.dma_start(
                    out=wstage[:], in_=W["L1a_wk"][c * 128:(c + 1) * 128, :]), "dml", 16)
            ctx.fence(["v"], ["dml"])
            @sect(ctx, "v")
            def _(v, c=c):
                ctx.mark(v.tensor_copy(wqk[:, DC * D + c * D: DC * D + (c + 1) * D],
                                       wstage[:]), "v")
        wq = wqk[:, 0:DC * D]
        wk = wqk[:, DC * D:2 * DC * D]
        ctx.fence(["t"], ["s", "v"])
        for which, wmat, dstbuf in (("q", wq, qf), ("k", wk, kf)):
            for oc in range(DC):
                @sect(ctx, "t")
                def _(te, wmat=wmat, oc=oc):
                    for c in range(DC):
                        for col in range(4):
                            inst = te.matmul(
                                S[:, col * 512:(col + 1) * 512],
                                wmat[:, c * D + oc * 128: c * D + (oc + 1) * 128],
                                lnf[:, c * SEQ + col * 512: c * SEQ + (col + 1) * 512],
                                start=(c == 0), stop=(c == DC - 1))
                    ctx.mark(inst, "t")
                ctx.fence(["s"], ["t"])
                @sect(ctx, "s")
                def _(sc, which=which, dstbuf=dstbuf, oc=oc):
                    scl = 0.125 if which == "q" else 1.0
                    inst = sc.activation(dstbuf[:, oc * SEQ:(oc + 1) * SEQ],
                                         S[:, :], AF.Identity, scale=scl)
                    ctx.mark(inst, "s")
                ctx.fence(["t"], ["s"])

        es_b.close()
        wob = es.enter_context(nc.sbuf_tensor(ctx.nm("a1_wob"), [128, DC * D], BF16))
        load_weight_bf16(ctx, W["L1a_wo"], wob, wstage, DC, D)
        wo = wob
        for qt in range(NT_SEQ):
            nk = qt + 1
            ncol = (nk * 128 + 511) // 512
            for h in range(16):
                oc, rb = h // 2, 64 * (h % 2)
                ctx.fence(["t"], ["v", "s"])
                @sect(ctx, "t")
                def _(te, oc=oc, rb=rb, qt=qt, nk=nk, ncol=ncol):
                    lhs = qf[rb:rb + 64, oc * SEQ + qt * 128: oc * SEQ + (qt + 1) * 128]
                    for col in range(ncol):
                        hi = min((col + 1) * 512, nk * 128)
                        inst = te.matmul(
                            S[:, col * 512:hi], lhs,
                            kf[rb:rb + 64, oc * SEQ + col * 512: oc * SEQ + hi],
                            start=True, stop=True)
                    ctx.mark(inst, "t")
                ctx.fence(["v"], ["t"])
                @sect(ctx, "v")
                def _(v, qt=qt, nk=nk):
                    v.tensor_add(S[:, qt * 128:(qt + 1) * 128],
                                 S[:, qt * 128:(qt + 1) * 128], cmask[:])
                    v.drain()
                    v.reduce_max(row[:, 0:1], S[:, 0:nk * 128], axis=AX.X)
                    v.drain()
                    ctx.mark(v.tensor_scalar_mul(row[:, 1:2], row[:, 0:1], -1.0), "v")
                ctx.fence(["s"], ["v"])
                @sect(ctx, "s")
                def _(sc, nk=nk):
                    ctx.mark(sc.activation(P[:, 0:nk * 128], S[:, 0:nk * 128], AF.Exp,
                                           bias=row[:, 1:2], accum_out=row[:, 2:3]), "s")
                ctx.fence(["v"], ["s"])
                @sect(ctx, "v")
                def _(v):
                    ctx.mark(v.reciprocal(row[:, 3:4], row[:, 2:3]), "v")
                for b in range((nk + 7) // 8):
                    ctx.fence(["t"], ["s", "v"])
                    @sect(ctx, "t")
                    def _(te, b=b, nk=nk):
                        for k in range(min(8, nk - b * 8)):
                            kc = b * 8 + k
                            inst = te.transpose(ptp[:, k * 128:(k + 1) * 128],
                                                P[:, kc * 128:(kc + 1) * 128], ident[:])
                        ctx.mark(inst, "t")
                    ctx.fence(["s"], ["t"])
                    @sect(ctx, "s")
                    def _(sc, b=b, nk=nk):
                        for k in range(min(8, nk - b * 8)):
                            kc = b * 8 + k
                            inst = sc.copy(Pt[:, kc * 128:(kc + 1) * 128],
                                           ptp[:, k * 128:(k + 1) * 128])
                        ctx.mark(inst, "s")
                ctx.fence(["t"], ["s"])
                @sect(ctx, "t")
                def _(te, h=h, nk=nk):
                    for kc in range(nk):
                        inst = te.matmul(mm[:, 0:64], Pt[:, kc * 128:(kc + 1) * 128],
                                         vt[:, kc * D + h * 64: kc * D + (h + 1) * 64],
                                         start=(kc == 0), stop=(kc == nk - 1))
                    ctx.mark(inst, "t")
                ctx.fence(["s"], ["t", "v"])
                @sect(ctx, "s")
                def _(sc, h=h):
                    ctx.mark(sc.activation(attn[:, h * 64:(h + 1) * 64], mm[:, 0:64],
                                           AF.Identity, scale=row[:, 3:4]), "s")
                ctx.fence(["t"], ["s"])

            ctx.fence(["t"], ["s"])
            @sect(ctx, "t")
            def _(te):
                for c in range(DC):
                    inst = te.transpose(ptp[:, c * 128:(c + 1) * 128],
                                        attn[:, c * 128:(c + 1) * 128], ident[:])
                ctx.mark(inst, "t")
            ctx.fence(["s"], ["t"])
            @sect(ctx, "s")
            def _(sc):
                for c in range(DC):
                    inst = sc.copy(atf[:, c * 128:(c + 1) * 128], ptp[:, c * 128:(c + 1) * 128])
                ctx.mark(inst, "s")
            ctx.fence(["y"], ["s", "g"])
            @sect(ctx, "y")
            def _(sync, qt=qt):
                ctx.mark(sync.dma_start(
                    out=xt[:], in_=xres[tok0 + qt * 128: tok0 + (qt + 1) * 128, :]), "dml", 16)
            ctx.fence(["t"], ["s", "v"])
            @sect(ctx, "t")
            def _(te):
                for c in range(DC):
                    for col in range(2):
                        inst = te.matmul(mm[:, col * 512:(col + 1) * 512],
                                         atf[:, c * 128:(c + 1) * 128],
                                         wo[:, c * D + col * 512: c * D + (col + 1) * 512],
                                         start=(c == 0), stop=(c == DC - 1))
                ctx.mark(inst, "t")
            ctx.fence(["v"], ["t", "dml", "dms"])
            @sect(ctx, "v")
            def _(v):
                ctx.mark(v.tensor_add(eres[:], xt[:], mm[:, :]), "v")
            ctx.fence(["g"], ["v"])
            @sect(ctx, "g")
            def _(g, qt=qt):
                ctx.mark(g.dma_start(
                    out=xres[tok0 + qt * 128: tok0 + (qt + 1) * 128, :], in_=eres[:]), "dms", 16)


def emit_l1_ffn(ctx, xres, W):
    nc = ctx.nc
    es = contextlib.ExitStack()
    with es:
        ident = es.enter_context(nc.sbuf_tensor(ctx.nm("f1_id"), [128, 128], BF16))
        gate_w = es.enter_context(nc.sbuf_tensor(ctx.nm("f1_gw"), [128, DC * 8], BF16))
        wstage = es.enter_context(nc.sbuf_tensor(ctx.nm("f1_ws"), [128, D], F32))
        stats = es.enter_context(nc.sbuf_tensor(ctx.nm("f1_st"), [128, NT_SEQ * 8], F32))

        @sect(ctx, "y")
        def _(sync):
            ctx.mark(sync.dma_start(out=ident[:], in_=W["ident"][:, :]), "dml", 16)
        load_weight_bf16(ctx, W["L1f_gate"], gate_w, wstage, DC, 8)

        for s in range(NSEQ):
            _l1f_one_seq(ctx, s, xres, W, ident, gate_w, wstage, stats)


def _l1f_one_seq(ctx, s, xres, W, ident, gate_w, wstage, stats):
    nc = ctx.nc
    tok0 = s * SEQ
    es = contextlib.ExitStack()
    with es:
        lnf = es.enter_context(nc.sbuf_tensor(ctx.nm("f1_lnf"), [128, DC * SEQ], BF16))
        yacc = es.enter_context(nc.sbuf_tensor(ctx.nm("f1_ya"), [128, NT_SEQ * D], F32))
        scl = es.enter_context(nc.sbuf_tensor(ctx.nm("f1_sc"), [128, NT_SEQ * 8], F32))
        xt = es.enter_context(nc.sbuf_tensor(ctx.nm("f1_xt"), [128, D], F32))
        sq = es.enter_context(nc.sbuf_tensor(ctx.nm("f1_sq"), [128, D], F32))
        lnb = es.enter_context(nc.sbuf_tensor(ctx.nm("f1_lnb"), [128, D], BF16))
        mk = es.enter_context(nc.sbuf_tensor(ctx.nm("f1_mk"), [128, 48], F32))
        h1 = es.enter_context(nc.sbuf_tensor(ctx.nm("f1_h1"), [128, 512], F32))
        hb = es.enter_context(nc.sbuf_tensor(ctx.nm("f1_hb"), [128, 512], BF16))
        hf = es.enter_context(nc.sbuf_tensor(ctx.nm("f1_hf"), [128, 512], BF16))
        we = es.enter_context(nc.sbuf_tensor(ctx.nm("f1_we"), [128, 2 * 4096 + 4 * D], BF16))
        eres = es.enter_context(nc.sbuf_tensor(ctx.nm("f1_er"), [128, D], F32))
        pu1 = es.enter_context(nc.psum_tensor(ctx.nm("f1_p1"), [128, 512], F32))
        pu3 = es.enter_context(nc.psum_tensor(ctx.nm("f1_p3"), [128, 512], F32))
        py = es.enter_context(nc.psum_tensor(ctx.nm("f1_py"), [128, D], F32))
        ptp = es.enter_context(nc.psum_tensor(ctx.nm("f1_ptp"), [128, 1024], BF16))

        for t in range(NT_SEQ):
            st = stats[:, t * 8:(t + 1) * 8]
            ctx.fence(["y"], ["v", "s", "t"])
            @sect(ctx, "y")
            def _(sync, t=t):
                ctx.mark(sync.dma_start(
                    out=xt[:], in_=xres[tok0 + t * 128: tok0 + (t + 1) * 128, :]), "dml", 16)
            ctx.fence(["v", "s"], ["dml"])
            emit_ln_stats(ctx, xt[:], st, sq[:], rms=False)
            ctx.fence(["s"], ["v"])
            @sect(ctx, "s")
            def _(sc, st=st):
                ctx.mark(sc.activation(lnb[:], xt[:], AF.Identity,
                                       bias=st[:, 2:3], scale=st[:, 1:2]), "s")
            ctx.fence(["t"], ["s"])
            @sect(ctx, "t")
            def _(te, t=t):
                for c in range(DC):
                    inst = te.transpose(ptp[:, c * 128:(c + 1) * 128],
                                        lnb[:, c * 128:(c + 1) * 128], ident[:])
                ctx.mark(inst, "t")
            ctx.fence(["s"], ["t"])
            @sect(ctx, "s")
            def _(sc, t=t):
                for c in range(DC):
                    inst = sc.copy(lnf[:, c * SEQ + t * 128: c * SEQ + (t + 1) * 128],
                                   ptp[:, c * 128:(c + 1) * 128])
                ctx.mark(inst, "s")
            ctx.fence(["t"], ["s"])
            @sect(ctx, "t")
            def _(te, t=t):
                for c in range(DC):
                    inst = te.matmul(pu1[:, 0:8],
                                     lnf[:, c * SEQ + t * 128: c * SEQ + (t + 1) * 128],
                                     gate_w[:, c * 8:(c + 1) * 8],
                                     start=(c == 0), stop=(c == DC - 1))
                ctx.mark(inst, "t")
            ctx.fence(["v"], ["t"])
            @sect(ctx, "v")
            def _(v):
                v.reduce_max(mk[:, 0:1], pu1[:, 0:8], axis=AX.X)
                v.drain()
                ctx.mark(v.tensor_scalar_mul(mk[:, 1:2], mk[:, 0:1], -1.0), "v")
            ctx.fence(["s"], ["v"])
            @sect(ctx, "s")
            def _(sc):
                ctx.mark(sc.activation(mk[:, 8:16], pu1[:, 0:8], AF.Exp,
                                       bias=mk[:, 1:2], accum_out=mk[:, 2:3]), "s")
            ctx.fence(["v"], ["s"])
            @sect(ctx, "v")
            def _(v, t=t):
                v.reciprocal(mk[:, 3:4], mk[:, 2:3])
                v.drain()
                v.tensor_scalar(mk[:, 8:16], mk[:, 8:16], mk[:, 3:4], None, ALU.mult)
                v.drain()
                v.tensor_reduce(mk[:, 4:5], mk[:, 8:16], axis=AX.X, op=ALU.max)
                v.drain()
                m1b = mk[:, 4:5].broadcast_to([128, 8])
                v.tensor_tensor(mk[:, 16:24], mk[:, 8:16], m1b, ALU.is_equal)
                v.drain()
                v.scalar_tensor_tensor(mk[:, 16:24], mk[:, 16:24], -1e30,
                                       mk[:, 8:16], ALU.mult, ALU.add)
                v.drain()
                v.tensor_reduce(mk[:, 4:5], mk[:, 16:24], axis=AX.X, op=ALU.max)
                v.drain()
                v.tensor_tensor(mk[:, 16:24], mk[:, 8:16], m1b, ALU.is_ge)
                v.drain()
                inst = v.tensor_mul(scl[:, t * 8:(t + 1) * 8], mk[:, 16:24], mk[:, 8:16])
                ctx.mark(inst, "v")
            @sect(ctx, "v")
            def _(v, t=t):
                ctx.mark(v.memset(yacc[:, t * D:(t + 1) * D], 0.0), "v")

        for e in range(8):
            for wi, (name, wcols) in enumerate((("L1f_w1", 512), ("L1f_w3", 512))):
                for c in range(DC):
                    ctx.fence(["y"], ["v", "s", "t"])
                    @sect(ctx, "y")
                    def _(sync, name=name, c=c, e=e):
                        ctx.mark(sync.dma_start(
                            out=wstage[:, 0:512],
                            in_=W[name][e, c * 128:(c + 1) * 128, :]), "dml", 16)
                    ctx.fence(["v"], ["dml"])
                    @sect(ctx, "v")
                    def _(v, wi=wi, c=c):
                        ctx.mark(v.tensor_copy(
                            we[:, wi * 4096 + c * 512: wi * 4096 + (c + 1) * 512],
                            wstage[:, 0:512]), "v")
            for c in range(4):
                ctx.fence(["y"], ["v", "s", "t"])
                @sect(ctx, "y")
                def _(sync, c=c, e=e):
                    ctx.mark(sync.dma_start(
                        out=wstage[:, 0:D],
                        in_=W["L1f_w2"][e, c * 128:(c + 1) * 128, :]), "dml", 16)
                ctx.fence(["v"], ["dml"])
                @sect(ctx, "v")
                def _(v, c=c):
                    ctx.mark(v.tensor_copy(we[:, 8192 + c * D: 8192 + (c + 1) * D],
                                           wstage[:, 0:D]), "v")

            for t in range(NT_SEQ):
                cap = scl[:, t * 8 + e: t * 8 + e + 1]
                ctx.fence(["t"], ["v", "s"])
                @sect(ctx, "t")
                def _(te, t=t, e=e):
                    for c in range(DC):
                        inst = te.matmul(pu1[:],
                                         lnf[:, c * SEQ + t * 128: c * SEQ + (t + 1) * 128],
                                         we[:, c * 512:(c + 1) * 512],
                                         start=(c == 0), stop=(c == DC - 1))
                    for c in range(DC):
                        inst = te.matmul(pu3[:],
                                         lnf[:, c * SEQ + t * 128: c * SEQ + (t + 1) * 128],
                                         we[:, 4096 + c * 512: 4096 + (c + 1) * 512],
                                         start=(c == 0), stop=(c == DC - 1))
                    ctx.mark(inst, "t")
                ctx.fence(["s"], ["t"])
                @sect(ctx, "s")
                def _(sc, cap=cap):
                    sc.activation(h1[:], pu1[:], AF.Silu, scale=cap)
                    ctx.mark(sc.activation(hb[:], pu3[:], AF.Identity, scale=cap), "s")
                ctx.fence(["v"], ["s"])
                @sect(ctx, "v")
                def _(v):
                    ctx.mark(v.tensor_mul(hb[:], h1[:], hb[:]), "v")
                ctx.fence(["t"], ["v"])
                @sect(ctx, "t")
                def _(te):
                    for c in range(4):
                        inst = te.transpose(ptp[:, c * 128:(c + 1) * 128],
                                            hb[:, c * 128:(c + 1) * 128], ident[:])
                    ctx.mark(inst, "t")
                ctx.fence(["s"], ["t"])
                @sect(ctx, "s")
                def _(sc):
                    for c in range(4):
                        inst = sc.copy(hf[:, c * 128:(c + 1) * 128],
                                       ptp[:, c * 128:(c + 1) * 128])
                    ctx.mark(inst, "s")
                ctx.fence(["t"], ["s"])
                @sect(ctx, "t")
                def _(te):
                    for c in range(4):
                        for col in range(2):
                            inst = te.matmul(py[:, col * 512:(col + 1) * 512],
                                             hf[:, c * 128:(c + 1) * 128],
                                             we[:, 8192 + c * D + col * 512:
                                                8192 + c * D + (col + 1) * 512],
                                             start=(c == 0), stop=(c == 3))
                    ctx.mark(inst, "t")
                ctx.fence(["v"], ["t"])
                @sect(ctx, "v")
                def _(v, t=t):
                    ctx.mark(v.tensor_add(yacc[:, t * D:(t + 1) * D],
                                          yacc[:, t * D:(t + 1) * D], py[:]), "v")

        for t in range(NT_SEQ):
            ctx.fence(["y"], ["v", "g"])
            @sect(ctx, "y")
            def _(sync, t=t):
                ctx.mark(sync.dma_start(
                    out=xt[:], in_=xres[tok0 + t * 128: tok0 + (t + 1) * 128, :]), "dml", 16)
            ctx.fence(["v"], ["dml", "dms"])
            @sect(ctx, "v")
            def _(v, t=t):
                ctx.mark(v.tensor_add(eres[:], xt[:], yacc[:, t * D:(t + 1) * D]), "v")
            ctx.fence(["g"], ["v"])
            @sect(ctx, "g")
            def _(g, t=t):
                ctx.mark(g.dma_start(
                    out=xres[tok0 + t * 128: tok0 + (t + 1) * 128, :], in_=eres[:]), "dms", 16)


L2_STD = [1.0, 2.0, 4.0, 8.0, 1.0, 2.0, 4.0, 8.0]
L2_OFF = [-1.0, -2.0, -4.0, -8.0, -1.0, -2.0, -4.0, -8.0]


def emit_l2_attn(ctx, xres, W):
    import math as _math
    nc = ctx.nc
    es = contextlib.ExitStack()
    with es:
        ident = es.enter_context(nc.sbuf_tensor(ctx.nm("a2_id"), [128, 128], BF16))
        win = es.enter_context(nc.sbuf_tensor(ctx.nm("a2_wi"), [128, DC * D], BF16))
        wout = es.enter_context(nc.sbuf_tensor(ctx.nm("a2_wo"), [128, DC * D], BF16))
        wstage = es.enter_context(nc.sbuf_tensor(ctx.nm("a2_ws"), [128, D], F32))
        stats = es.enter_context(nc.sbuf_tensor(ctx.nm("a2_st"), [128, NT_SEQ * 8], F32))
        gt = es.enter_context(nc.sbuf_tensor(ctx.nm("a2_gt"), [128, 640], F32))
        gb = es.enter_context(nc.sbuf_tensor(ctx.nm("a2_gb"), [128, 640], BF16))
        ptg = es.enter_context(nc.sbuf_tensor(ctx.nm("a2_ptg"), [128, 8 * 640], BF16))
        ptp0 = es.enter_context(nc.psum_tensor(ctx.nm("a2_pt0"), [128, 1024], BF16))

        @sect(ctx, "y")
        def _(sync):
            ctx.mark(sync.dma_start(out=ident[:], in_=W["ident"][:, :]), "dml", 16)
        load_weight_bf16(ctx, W["L2a_win"], win, wstage, DC, D)
        load_weight_bf16(ctx, W["L2a_wout"], wout, wstage, DC, D)

        for h in range(8):
            std, off = L2_STD[h], L2_OFF[h]
            ctx.fence(["g"], ["t", "s", "v"])
            @sect(ctx, "g")
            def _(g, off=off):
                ctx.mark(g.iota(gt[:], [[1, 640]], base=int(-256 - off),
                                channel_multiplier=-1,
                                allow_small_or_imprecise_dtypes=True), "g")
            ctx.fence(["s"], ["g"])
            @sect(ctx, "s")
            def _(sc, std=std):
                sc.activation(gt[:], gt[:], AF.Square, scale=1.0 / std)
                ctx.mark(sc.activation(gt[:], gt[:], AF.Exp, scale=-0.5), "s")
            ctx.fence(["v"], ["s"])
            @sect(ctx, "v")
            def _(v, std=std):
                inv_norm = 1.0 / (std * _math.sqrt(2.0 * _math.pi))
                ctx.mark(v.tensor_scalar_mul(gb[:], gt[:], inv_norm), "v")
            ctx.fence(["t"], ["v"])
            @sect(ctx, "t")
            def _(te):
                for r in range(5):
                    inst = te.transpose(ptp0[:, r * 128:(r + 1) * 128],
                                        gb[:, r * 128:(r + 1) * 128], ident[:])
                ctx.mark(inst, "t")
            ctx.fence(["s"], ["t"])
            @sect(ctx, "s")
            def _(sc, h=h):
                for r in range(5):
                    inst = sc.copy(ptg[:, h * 640 + r * 128: h * 640 + (r + 1) * 128],
                                   ptp0[:, r * 128:(r + 1) * 128])
                ctx.mark(inst, "s")

        for s in range(NSEQ):
            _l2a_one_seq(ctx, s, xres, W, ident, win, wout, stats, ptg)


def _l2a_one_seq(ctx, s, xres, W, ident, win, wout, stats, ptg):
    nc = ctx.nc
    tok0 = s * SEQ
    es = contextlib.ExitStack()
    with es:
        lnf = es.enter_context(nc.sbuf_tensor(ctx.nm("a2_lnf"), [128, DC * SEQ], BF16))
        vt = es.enter_context(nc.sbuf_tensor(ctx.nm("a2_vt"), [128, NT_SEQ * D], BF16))
        xt = es.enter_context(nc.sbuf_tensor(ctx.nm("a2_xt"), [128, D], F32))
        sq = es.enter_context(nc.sbuf_tensor(ctx.nm("a2_sq"), [128, D], F32))
        lnb = es.enter_context(nc.sbuf_tensor(ctx.nm("a2_lnb"), [128, D], BF16))
        att = es.enter_context(nc.sbuf_tensor(ctx.nm("a2_at"), [128, D], BF16))
        atf = es.enter_context(nc.sbuf_tensor(ctx.nm("a2_atf"), [128, D], BF16))
        eres = es.enter_context(nc.sbuf_tensor(ctx.nm("a2_er"), [128, D], F32))
        mm = es.enter_context(nc.psum_tensor(ctx.nm("a2_mm"), [128, D], F32))
        pa = es.enter_context(nc.psum_tensor(ctx.nm("a2_pa"), [128, D], F32))
        ptp = es.enter_context(nc.psum_tensor(ctx.nm("a2_ptp"), [128, 1024], BF16))

        for t in range(NT_SEQ):
            st = stats[:, t * 8:(t + 1) * 8]
            ctx.fence(["y"], ["v", "s", "t"])
            @sect(ctx, "y")
            def _(sync, t=t):
                ctx.mark(sync.dma_start(
                    out=xt[:], in_=xres[tok0 + t * 128: tok0 + (t + 1) * 128, :]), "dml", 16)
            ctx.fence(["v", "s"], ["dml"])
            emit_ln_stats(ctx, xt[:], st, sq[:], rms=False)
            ctx.fence(["s"], ["v"])
            @sect(ctx, "s")
            def _(sc, st=st):
                ctx.mark(sc.activation(lnb[:], xt[:], AF.Identity,
                                       bias=st[:, 2:3], scale=st[:, 1:2]), "s")
            ctx.fence(["t"], ["s"])
            @sect(ctx, "t")
            def _(te):
                for c in range(DC):
                    inst = te.transpose(ptp[:, c * 128:(c + 1) * 128],
                                        lnb[:, c * 128:(c + 1) * 128], ident[:])
                ctx.mark(inst, "t")
            ctx.fence(["s"], ["t"])
            @sect(ctx, "s")
            def _(sc, t=t):
                for c in range(DC):
                    inst = sc.copy(lnf[:, c * SEQ + t * 128: c * SEQ + (t + 1) * 128],
                                   ptp[:, c * 128:(c + 1) * 128])
                ctx.mark(inst, "s")
            ctx.fence(["t"], ["s", "v"])
            @sect(ctx, "t")
            def _(te, t=t):
                for col in range(2):
                    for c in range(DC):
                        inst = te.matmul(
                            mm[:, col * 512:(col + 1) * 512],
                            lnf[:, c * SEQ + t * 128: c * SEQ + (t + 1) * 128],
                            win[:, c * D + col * 512: c * D + (col + 1) * 512],
                            start=(c == 0), stop=(c == DC - 1))
                ctx.mark(inst, "t")
            ctx.fence(["v"], ["t"])
            @sect(ctx, "v")
            def _(v, t=t):
                ctx.mark(v.tensor_copy(vt[:, t * D:(t + 1) * D], mm[:, :]), "v")

        for qt in range(NT_SEQ):
            ks = [kc for kc in range(qt - 2, qt + 3) if 0 <= kc < NT_SEQ]
            ctx.fence(["t"], ["v", "s"])
            @sect(ctx, "t")
            def _(te, qt=qt, ks=ks):
                for h in range(8):
                    for i, kc in enumerate(ks):
                        r = kc - (qt - 2)
                        inst = te.matmul(
                            pa[:, h * 128:(h + 1) * 128],
                            ptg[:, h * 640 + r * 128: h * 640 + (r + 1) * 128],
                            vt[:, kc * D + h * 128: kc * D + (h + 1) * 128],
                            start=(i == 0), stop=(i == len(ks) - 1))
                ctx.mark(inst, "t")
            ctx.fence(["s"], ["t"])
            @sect(ctx, "s")
            def _(sc):
                ctx.mark(sc.copy(att[:], pa[:]), "s")
            ctx.fence(["t"], ["s"])
            @sect(ctx, "t")
            def _(te):
                for c in range(DC):
                    inst = te.transpose(ptp[:, c * 128:(c + 1) * 128],
                                        att[:, c * 128:(c + 1) * 128], ident[:])
                ctx.mark(inst, "t")
            ctx.fence(["s"], ["t"])
            @sect(ctx, "s")
            def _(sc):
                for c in range(DC):
                    inst = sc.copy(atf[:, c * 128:(c + 1) * 128],
                                   ptp[:, c * 128:(c + 1) * 128])
                ctx.mark(inst, "s")
            ctx.fence(["y"], ["s", "g"])
            @sect(ctx, "y")
            def _(sync, qt=qt):
                ctx.mark(sync.dma_start(
                    out=xt[:], in_=xres[tok0 + qt * 128: tok0 + (qt + 1) * 128, :]), "dml", 16)
            ctx.fence(["t"], ["s", "v"])
            @sect(ctx, "t")
            def _(te):
                for c in range(DC):
                    for col in range(2):
                        inst = te.matmul(mm[:, col * 512:(col + 1) * 512],
                                         atf[:, c * 128:(c + 1) * 128],
                                         wout[:, c * D + col * 512: c * D + (col + 1) * 512],
                                         start=(c == 0), stop=(c == DC - 1))
                ctx.mark(inst, "t")
            ctx.fence(["v"], ["t", "dml", "dms"])
            @sect(ctx, "v")
            def _(v):
                ctx.mark(v.tensor_add(eres[:], xt[:], mm[:, :]), "v")
            ctx.fence(["g"], ["v"])
            @sect(ctx, "g")
            def _(g, qt=qt):
                ctx.mark(g.dma_start(
                    out=xres[tok0 + qt * 128: tok0 + (qt + 1) * 128, :], in_=eres[:]), "dms", 16)


def _round_signed(ctx, v, q, ti, tf, c1, positive=False):
    v.drain()
    v.tensor_copy(ti, q)
    v.drain()
    v.tensor_copy(tf, ti)
    v.drain()
    v.scalar_tensor_tensor(c1, q, 0.5, tf, ALU.subtract, ALU.is_ge)
    v.drain()
    inst = v.tensor_add(tf, tf, c1)
    if not positive:
        v.drain()
        v.scalar_tensor_tensor(c1, q, 0.5, tf, ALU.add, ALU.is_le)
        v.drain()
        inst = v.tensor_sub(tf, tf, c1)
    return inst


def emit_l2_ffn(ctx, xres, W, ccbufs, dbg=False):
    nc = ctx.nc
    if dbg:
        dbg_sc = nc.dram_tensor("dbg_sc", [128, 32], F32, kind="ExternalOutput").ap()
        dbg_up = nc.dram_tensor("dbg_up", [128, 3072], BF16, kind="ExternalOutput").ap()
        dbg_g = nc.dram_tensor("dbg_g", [128, 3072], BF16, kind="ExternalOutput").ap()
        dbg_w = nc.dram_tensor("dbg_w", [128, 3072], BF16, kind="ExternalOutput").ap()
        dbg_xh = nc.dram_tensor("dbg_xh", [128, 1024], BF16, kind="ExternalOutput").ap()
        dbg_g2 = nc.dram_tensor("dbg_g2", [128, 3072], BF16, kind="ExternalOutput").ap()
        dbg_o = nc.dram_tensor("dbg_o", [128, 3072], BF16, kind="ExternalOutput").ap()
    es = contextlib.ExitStack()
    with es:
        ident = es.enter_context(nc.sbuf_tensor(ctx.nm("f2_id"), [128, 128], BF16))
        identf = es.enter_context(nc.sbuf_tensor(ctx.nm("f2_idf"), [128, 128], F32))
        ones = es.enter_context(nc.sbuf_tensor(ctx.nm("f2_on"), [1, 128], F32))
        stats = es.enter_context(nc.sbuf_tensor(ctx.nm("f2_st"), [128, 2 * NT_SEQ * 8], F32))
        scal = es.enter_context(nc.sbuf_tensor(ctx.nm("f2_scal"), [128, 16], F32))
        sc1 = es.enter_context(nc.sbuf_tensor(ctx.nm("f2_sc1"), [1, 16], F32))
        xt = es.enter_context(nc.sbuf_tensor(ctx.nm("f2_xt"), [128, D], F32))
        run = es.enter_context(nc.sbuf_tensor(ctx.nm("f2_run"), [128, 8], F32))
        runb = es.enter_context(nc.sbuf_tensor(ctx.nm("f2_runb"), [128, 128], F32))
        eres = es.enter_context(nc.sbuf_tensor(ctx.nm("f2_er"), [128, D], F32))
        m1b = es.enter_context(nc.sbuf_tensor(ctx.nm("f2_m1b"), [128, 3072], BF16))
        pmu = es.enter_context(nc.psum_tensor(ctx.nm("f2_pmu"), [128, 3072], F32))
        pt1 = es.enter_context(nc.psum_tensor(ctx.nm("f2_pt1"), [128, 1024], BF16))

        @sect(ctx, "y")
        def _(sync):
            ctx.mark(sync.dma_start(out=ident[:], in_=W["ident"][:, :]), "dml", 16)
        @sect(ctx, "v")
        def _(v):
            v.memset(ones[:], 1.0)
            v.memset(run[:], 0.0)
            v.memset(runb[:], 0.0)
            ctx.mark(v.tensor_copy(identf[:], ident[:]), "v")

        def part_reduce_max(src_col, dst_col):
            ctx.fence(["g"], ["v", "s", "t", "dml"])
            @sect(ctx, "g")
            def _(g, src_col=src_col):
                ctx.mark(g.dma_start(out=ccbufs["prscr"][0:1, :],
                                     in_=run[:, src_col:src_col + 1]), "dms", 16)
                g.wait_ge(ctx.sem["dms"], ctx.tot["dms"])
                ctx.mark(g.dma_start(out=runb[0:1, 0:128],
                                     in_=ccbufs["prscr"][0:1, :]), "dml", 16)
            ctx.fence(["v"], ["dml"])
            @sect(ctx, "v")
            def _(v, dst_col=dst_col):
                ctx.mark(v.reduce_max(sc1[0:1, dst_col:dst_col + 1],
                                      runb[0:1, 0:128], axis=AX.X), "v")

        def allreduce_max():
            ctx.fence(["g"], ["v", "s", "t"])
            @sect(ctx, "g")
            def _(g):
                ctx.mark(g.dma_start(out=ccbufs["cc_in"][0:1, 0:4],
                                     in_=sc1[0:1, 0:4]), "dms", 16)
                g.wait_ge(ctx.sem["dms"], ctx.tot["dms"])
                g.collective_compute(
                    "AllReduce", ALU.max,
                    replica_groups=[list(range(8))],
                    ins=[ccbufs["cc_in"].ap()],
                    outs=[ccbufs["cc_out"].ap()],
                ).then_inc(ctx.sem["cc"], 1)
                ctx.tot["cc"] += 1
                g.wait_ge(ctx.sem["cc"], ctx.tot["cc"])
                ctx.mark(g.dma_start(out=sc1[0:1, 0:4],
                                     in_=ccbufs["cc_out"][0:1, 0:4]), "dml", 16)
            ctx.fence(["v", "s"], ["dml"])

        def bcast_scal(src_col, dst_col):
            ctx.fence(["t"], ["v", "s", "g"])
            @sect(ctx, "t")
            def _(te, src_col=src_col, dst_col=dst_col):
                ctx.mark(te.matmul(pmu[:, 128:129], ones[:],
                                   sc1[0:1, src_col:src_col + 1],
                                   start=True, stop=True), "t")
            ctx.fence(["v"], ["t"])
            @sect(ctx, "v")
            def _(v, dst_col=dst_col):
                ctx.mark(v.tensor_copy(scal[:, dst_col:dst_col + 1],
                                       pmu[:, 128:129]), "v")

        def dequant_weight(srcw, nrows_chunks, width, dst, dst_chunk_width, scr):
            wst, wq, wi, wf, wc1, wsc, wbf = scr
            nblk = width // 32
            for r in range(nrows_chunks):
                ctx.fence(["y"], ["v", "s", "t", "g", "dms"])
                @sect(ctx, "y")
                def _(sync, r=r, srcw=srcw, width=width):
                    ctx.mark(sync.dma_start(
                        out=wst[:, 0:width], in_=srcw[r * 128:(r + 1) * 128, :]), "dml", 16)
                ctx.fence(["v"], ["dml"])
                @sect(ctx, "v")
                def _(v, width=width, nblk=nblk):
                    w3 = wst[:, 0:width].rearrange("p (b k) -> p b k", k=32)
                    v.tensor_reduce(wsc[:, 0:nblk], w3, axis=AX.X, op=ALU.max,
                                    apply_absolute_value=True)
                    v.drain()
                    v.tensor_scalar(wsc[:, 0:nblk], wsc[:, 0:nblk], 1.0 / 7.0, 1e-12,
                                    ALU.mult, ALU.max)
                    v.drain()
                    v.reciprocal(wsc[:, 96:96 + nblk], wsc[:, 0:nblk])
                    v.drain()
                    rb = wsc[:, 96:96 + nblk].rearrange(
                        "p (b k) -> p b k", k=1).broadcast_to([128, nblk, 32])
                    v.tensor_tensor(wq[:, 0:width].rearrange("p (b k) -> p b k", k=32),
                                    w3, rb, ALU.mult)
                    _round_signed(ctx, v, wq[:, 0:width], wi[:, 0:width],
                                  wf[:, 0:width], wc1[:, 0:width])
                    v.drain()
                    sb = wsc[:, 0:nblk].rearrange(
                        "p (b k) -> p b k", k=1).broadcast_to([128, nblk, 32])
                    inst = v.tensor_tensor(
                        wbf[:, 0:width].rearrange("p (b k) -> p b k", k=32),
                        wf[:, 0:width].rearrange("p (b k) -> p b k", k=32), sb, ALU.mult)
                    ctx.mark(inst, "v")
                for jb in range(0, width // 128, 8):
                    ctx.fence(["t"], ["v", "s"])
                    @sect(ctx, "t")
                    def _(te, jb=jb, width=width):
                        for j in range(jb, min(jb + 8, width // 128)):
                            inst = te.transpose(pt1[:, (j % 8) * 128:(j % 8 + 1) * 128],
                                                wbf[:, j * 128:(j + 1) * 128], ident[:])
                        ctx.mark(inst, "t")
                    ctx.fence(["s"], ["t"])
                    @sect(ctx, "s")
                    def _(sc, jb=jb, r=r, width=width, dst=dst,
                          dst_chunk_width=dst_chunk_width):
                        for j in range(jb, min(jb + 8, width // 128)):
                            inst = sc.copy(
                                dst[:, j * dst_chunk_width + r * 128:
                                    j * dst_chunk_width + (r + 1) * 128],
                                pt1[:, (j % 8) * 128:(j % 8 + 1) * 128])
                        ctx.mark(inst, "s")

        def mk_scr(stk, w):
            return (stk.enter_context(nc.sbuf_tensor(ctx.nm("f2_wst"), [128, w], F32)),
                    stk.enter_context(nc.sbuf_tensor(ctx.nm("f2_wq"), [128, w], F32)),
                    stk.enter_context(nc.sbuf_tensor(ctx.nm("f2_wi"), [128, w], I32)),
                    stk.enter_context(nc.sbuf_tensor(ctx.nm("f2_wf"), [128, w], F32)),
                    stk.enter_context(nc.sbuf_tensor(ctx.nm("f2_wc1"), [128, w], F32)),
                    stk.enter_context(nc.sbuf_tensor(ctx.nm("f2_wsc"), [128, 192], F32)),
                    stk.enter_context(nc.sbuf_tensor(ctx.nm("f2_wbf"), [128, w], BF16)))

        with contextlib.ExitStack() as es_w1:
            wupf = es_w1.enter_context(
                nc.sbuf_tensor(ctx.nm("f2_wupf"), [128, DC * 3072], BF16))
            wgf = es_w1.enter_context(
                nc.sbuf_tensor(ctx.nm("f2_wgf"), [128, DC * 3072], BF16))
            with contextlib.ExitStack() as es_scr:
                scr = mk_scr(es_scr, 3072)
                dequant_weight(W["L2f_wup"], 24, 1024, wupf, 3072, scr)
                dequant_weight(W["L2f_wgate"], 24, 1024, wgf, 3072, scr)
                ctx.fence(["v", "s", "t", "g", "y"])

            with contextlib.ExitStack() as es_ab:
                xhat = es_ab.enter_context(
                    nc.sbuf_tensor(ctx.nm("f2_xh"), [128, DC * NTOK], BF16))
                sq4 = es_ab.enter_context(nc.sbuf_tensor(ctx.nm("f2_sq4"), [128, D], F32))
                ri4 = es_ab.enter_context(nc.sbuf_tensor(ctx.nm("f2_ri4"), [128, D], I32))
                rf4 = es_ab.enter_context(nc.sbuf_tensor(ctx.nm("f2_rf4"), [128, D], F32))
                rc4 = es_ab.enter_context(nc.sbuf_tensor(ctx.nm("f2_rc4"), [128, D], F32))

                for t in range(2 * NT_SEQ):
                    st = stats[:, t * 8:(t + 1) * 8]
                    ctx.fence(["y"], ["v", "s", "t"])
                    @sect(ctx, "y")
                    def _(sync, t=t):
                        ctx.mark(sync.dma_start(
                            out=xt[:], in_=xres[t * 128:(t + 1) * 128, :]), "dml", 16)
                    ctx.fence(["v", "s"], ["dml"])
                    emit_ln_stats(ctx, xt[:], st, sq4[:], rms=False)
                    ctx.fence(["s"], ["v"])
                    @sect(ctx, "s")
                    def _(sc, st=st):
                        ctx.mark(sc.activation(sq4[:], xt[:], AF.Identity,
                                               bias=st[:, 2:3], scale=st[:, 1:2]), "s")
                    ctx.fence(["v"], ["s"])
                    @sect(ctx, "v")
                    def _(v):
                        v.reduce_max(run[:, 1:2], sq4[:], axis=AX.X,
                                     apply_absolute_value=True)
                        v.drain()
                        ctx.mark(v.tensor_max(run[:, 0:1], run[:, 0:1], run[:, 1:2]), "v")

                part_reduce_max(0, 0)
                allreduce_max()
                @sect(ctx, "v")
                def _(v):
                    v.tensor_copy(sc1[0:1, 7:8], sc1[0:1, 0:1])
                    v.drain()
                    v.tensor_scalar(sc1[0:1, 2:3], sc1[0:1, 0:1], 1.0 / 32767.0, 1e-12,
                                    ALU.mult, ALU.max)
                    v.drain()
                    v.tensor_copy(sc1[0:1, 9:10], sc1[0:1, 2:3])
                    v.drain()
                    ctx.mark(v.reciprocal(sc1[0:1, 2:3], sc1[0:1, 2:3]), "v")
                bcast_scal(2, 0)

                for t in range(2 * NT_SEQ):
                    st = stats[:, t * 8:(t + 1) * 8]
                    ctx.fence(["y"], ["v", "s", "t"])
                    @sect(ctx, "y")
                    def _(sync, t=t):
                        ctx.mark(sync.dma_start(
                            out=xt[:], in_=xres[t * 128:(t + 1) * 128, :]), "dml", 16)
                    ctx.fence(["s"], ["dml", "v"])
                    @sect(ctx, "s")
                    def _(sc, st=st):
                        ctx.mark(sc.activation(sq4[:], xt[:], AF.Identity,
                                               bias=st[:, 2:3], scale=st[:, 1:2]), "s")
                    ctx.fence(["v"], ["s"])
                    @sect(ctx, "v")
                    def _(v):
                        v.tensor_scalar(sq4[:], sq4[:], scal[:, 0:1], None, ALU.mult)
                        _round_signed(ctx, v, sq4[:], ri4[:], rf4[:], rc4[:])
                        v.drain()
                        ctx.mark(v.tensor_copy(m1b[:, 0:D], rf4[:]), "v")
                    ctx.fence(["t"], ["v"])
                    @sect(ctx, "t")
                    def _(te):
                        for c in range(DC):
                            inst = te.transpose(pt1[:, c * 128:(c + 1) * 128],
                                                m1b[:, c * 128:(c + 1) * 128], ident[:])
                        ctx.mark(inst, "t")
                    ctx.fence(["s"], ["t"])
                    @sect(ctx, "s")
                    def _(sc, t=t):
                        for c in range(DC):
                            inst = sc.copy(
                                xhat[:, c * NTOK + t * 128: c * NTOK + (t + 1) * 128],
                                pt1[:, c * 128:(c + 1) * 128])
                        ctx.mark(inst, "s")

                if dbg:
                    ctx.fence(["g"])
                    @sect(ctx, "g")
                    def _(g):
                        ctx.mark(g.dma_start(out=dbg_xh[:, :],
                                             in_=xhat[:, 0:1024]), "dms", 16)
                        g.wait_ge(ctx.sem["dms"], ctx.tot["dms"])
                for wmat, scname, runcol in ((wupf, "sc_up", 2), (wgf, "sc_g", 3)):
                    for t in range(2 * NT_SEQ):
                        ctx.fence(["t"], ["v", "s", "g"])
                        @sect(ctx, "t")
                        def _(te, t=t, wmat=wmat):
                            for col in range(6):
                                for c in range(DC):
                                    inst = te.matmul(
                                        pmu[:, col * 512:(col + 1) * 512],
                                        xhat[:, c * NTOK + t * 128:
                                             c * NTOK + (t + 1) * 128],
                                        wmat[:, c * 3072 + col * 512:
                                             c * 3072 + (col + 1) * 512],
                                        start=(c == 0), stop=(c == DC - 1))
                            ctx.mark(inst, "t")
                        ctx.fence(["v"], ["t", "dms"])
                        @sect(ctx, "v")
                        def _(v, runcol=runcol):
                            v.reduce_max(run[:, 1:2], pmu[:, :], axis=AX.X,
                                         apply_absolute_value=True)
                            v.drain()
                            v.tensor_max(run[:, runcol:runcol + 1],
                                         run[:, runcol:runcol + 1], run[:, 1:2])
                            ctx.mark(v.tensor_copy(m1b[:], pmu[:, :]), "v")
                        ctx.fence(["g"], ["v"])
                        @sect(ctx, "g")
                        def _(g, t=t, scname=scname):
                            ctx.mark(g.dma_start(
                                out=ccbufs[scname][t * 128:(t + 1) * 128, :],
                                in_=m1b[:]), "dms", 16)
                ctx.fence(["v", "s", "t", "g", "y"])

        part_reduce_max(2, 0)
        part_reduce_max(3, 1)
        allreduce_max()
        @sect(ctx, "v")
        def _(v):
            v.tensor_scalar(sc1[0:1, 2:4], sc1[0:1, 0:2], 1.0 / 32767.0, 1e-12,
                            ALU.mult, ALU.max)
            v.drain()
            v.reciprocal(sc1[0:1, 10:12], sc1[0:1, 2:4])
            v.tensor_mul(sc1[0:1, 12:13], sc1[0:1, 2:3], sc1[0:1, 9:10])
            v.drain()
            ctx.mark(v.tensor_mul(sc1[0:1, 13:14], sc1[0:1, 3:4], sc1[0:1, 9:10]), "v")
        bcast_scal(10, 1)
        bcast_scal(11, 2)
        bcast_scal(13, 3)

        wdnf = es.enter_context(nc.sbuf_tensor(ctx.nm("f2_wdnf"), [128, 24 * D], BF16))
        with contextlib.ExitStack() as es_scr2:
            scr2 = mk_scr(es_scr2, 3072)
            dequant_weight(W["L2f_wdown"], 8, 3072, wdnf, 1024, scr2)
            wst, wq, wi, wf, wc1, wsc, wbf = scr2
            m2b = es_scr2.enter_context(
                nc.sbuf_tensor(ctx.nm("f2_m2b"), [128, 3072], BF16))
            sq = es_scr2.enter_context(
                nc.sbuf_tensor(ctx.nm("f2_sqD"), [128, 3072], F32))

            for t in range(2 * NT_SEQ):
                ctx.fence(["y"], ["v", "s", "t", "g", "dms"])
                @sect(ctx, "y")
                def _(sync, t=t):
                    ctx.mark(sync.dma_start(
                        out=m1b[:], in_=ccbufs["sc_g"][t * 128:(t + 1) * 128, :]), "dml", 16)
                ctx.fence(["v"], ["dml"])
                @sect(ctx, "v")
                def _(v):
                    v.tensor_scalar(wq[:], m1b[:], scal[:, 2:3], None, ALU.mult)
                    inst = _round_signed(ctx, v, wq[:], wi[:], wf[:], wc1[:])
                    ctx.mark(inst, "v")
                ctx.fence(["s"], ["v"])
                @sect(ctx, "s")
                def _(sc):
                    ctx.mark(sc.activation(sq[:], wf[:], AF.Sigmoid,
                                           scale=scal[:, 3:4]), "s")
                ctx.fence(["v"], ["s"])
                @sect(ctx, "v")
                def _(v):
                    v.tensor_scalar_mul(wq[:], sq[:], 65536.0)
                    _round_signed(ctx, v, wq[:], wi[:], wst[:], wc1[:], positive=True)
                    v.drain()
                    v.tensor_scalar_min(wst[:], wst[:], 65535.0)
                    v.drain()
                    v.tensor_mul(wq[:], wf[:], wst[:])
                    v.drain()
                    v.reduce_max(run[:, 1:2], wq[:], axis=AX.X, apply_absolute_value=True)
                    v.drain()
                    v.tensor_max(run[:, 4:5], run[:, 4:5], run[:, 1:2])
                    ctx.mark(v.tensor_copy(m1b[:], wq[:]), "v")
                ctx.fence(["g"], ["v"])
                @sect(ctx, "g")
                def _(g, t=t):
                    ctx.mark(g.dma_start(
                        out=ccbufs["sc_g2"][t * 128:(t + 1) * 128, :], in_=m1b[:]), "dms", 16)

            part_reduce_max(4, 0)
            allreduce_max()
            @sect(ctx, "v")
            def _(v):
                v.tensor_scalar(sc1[0:1, 2:3], sc1[0:1, 0:1], 1.0 / 32767.0, 1e-12,
                                ALU.mult, ALU.max)
                v.drain()
                v.reciprocal(sc1[0:1, 10:11], sc1[0:1, 2:3])
                v.tensor_mul(sc1[0:1, 14:15], sc1[0:1, 13:14], sc1[0:1, 2:3])
                v.drain()
                ctx.mark(v.tensor_scalar_mul(sc1[0:1, 14:15], sc1[0:1, 14:15],
                                             1.0 / 65536.0), "v")
            bcast_scal(10, 4)

            for t in range(2 * NT_SEQ):
                ctx.fence(["y"], ["v", "s", "t", "g", "dms"])
                @sect(ctx, "y")
                def _(sync, t=t):
                    ctx.mark(sync.dma_start(
                        out=m1b[:], in_=ccbufs["sc_g2"][t * 128:(t + 1) * 128, :]), "dml", 16)
                    ctx.mark(sync.dma_start(
                        out=m2b[:], in_=ccbufs["sc_up"][t * 128:(t + 1) * 128, :]), "dml", 16)
                ctx.fence(["v"], ["dml"])
                @sect(ctx, "v")
                def _(v):
                    v.tensor_scalar(wq[:], m1b[:], scal[:, 4:5], None, ALU.mult)
                    _round_signed(ctx, v, wq[:], wi[:], wf[:], wc1[:])
                    v.drain()
                    v.tensor_scalar(wq[:], m2b[:], scal[:, 1:2], None, ALU.mult)
                    _round_signed(ctx, v, wq[:], wi[:], wst[:], wc1[:])
                    v.drain()
                    v.tensor_mul(wq[:], wf[:], wst[:])
                    v.drain()
                    v.reduce_max(run[:, 1:2], wq[:], axis=AX.X, apply_absolute_value=True)
                    v.drain()
                    v.tensor_max(run[:, 5:6], run[:, 5:6], run[:, 1:2])
                    ctx.mark(v.tensor_copy(m1b[:], wq[:]), "v")
                ctx.fence(["g"], ["v"])
                @sect(ctx, "g")
                def _(g, t=t):
                    ctx.mark(g.dma_start(
                        out=ccbufs["sc_o"][t * 128:(t + 1) * 128, :], in_=m1b[:]), "dms", 16)

            part_reduce_max(5, 0)
            allreduce_max()
            @sect(ctx, "v")
            def _(v):
                v.tensor_scalar(sc1[0:1, 2:3], sc1[0:1, 0:1], 1.0 / 32767.0, 1e-12,
                                ALU.mult, ALU.max)
                v.drain()
                v.reciprocal(sc1[0:1, 10:11], sc1[0:1, 2:3])
                v.tensor_mul(sc1[0:1, 15:16], sc1[0:1, 14:15], sc1[0:1, 12:13])
                v.drain()
                ctx.mark(v.tensor_mul(sc1[0:1, 15:16], sc1[0:1, 15:16],
                                      sc1[0:1, 2:3]), "v")
            bcast_scal(10, 5)
            bcast_scal(15, 6)

            for t in range(2 * NT_SEQ):
                ctx.fence(["y"], ["v", "s", "t", "g", "dms"])
                @sect(ctx, "y")
                def _(sync, t=t):
                    ctx.mark(sync.dma_start(
                        out=m1b[:], in_=ccbufs["sc_o"][t * 128:(t + 1) * 128, :]), "dml", 16)
                    ctx.mark(sync.dma_start(
                        out=xt[:], in_=xres[t * 128:(t + 1) * 128, :]), "dml", 16)
                ctx.fence(["v"], ["dml", "dms"])
                @sect(ctx, "v")
                def _(v):
                    v.tensor_scalar(wq[:], m1b[:], scal[:, 5:6], None, ALU.mult)
                    _round_signed(ctx, v, wq[:], wi[:], wf[:], wc1[:])
                    v.drain()
                    ctx.mark(v.tensor_copy(m2b[:], wf[:]), "v")
                for jb in range(0, 24, 8):
                    ctx.fence(["t"], ["v", "s"])
                    @sect(ctx, "t")
                    def _(te, jb=jb):
                        for j in range(jb, jb + 8):
                            inst = te.transpose(pt1[:, (j % 8) * 128:(j % 8 + 1) * 128],
                                                m2b[:, j * 128:(j + 1) * 128], ident[:])
                        ctx.mark(inst, "t")
                    ctx.fence(["s"], ["t"])
                    @sect(ctx, "s")
                    def _(sc, jb=jb):
                        for j in range(jb, jb + 8):
                            inst = sc.copy(m1b[:, j * 128:(j + 1) * 128],
                                           pt1[:, (j % 8) * 128:(j % 8 + 1) * 128])
                        ctx.mark(inst, "s")
                ctx.fence(["t"], ["s"])
                @sect(ctx, "t")
                def _(te):
                    for c in range(24):
                        for col in range(2):
                            inst = te.matmul(
                                pmu[:, col * 512:(col + 1) * 512],
                                m1b[:, c * 128:(c + 1) * 128],
                                wdnf[:, c * D + col * 512: c * D + (col + 1) * 512],
                                start=(c == 0), stop=(c == 23))
                    ctx.mark(inst, "t")
                ctx.fence(["v"], ["t"])
                @sect(ctx, "v")
                def _(v):
                    ctx.mark(v.scalar_tensor_tensor(eres[:], pmu[:, 0:D], scal[:, 6:7],
                                                    xt[:], ALU.mult, ALU.add), "v")
                ctx.fence(["g"], ["v"])
                @sect(ctx, "g")
                def _(g, t=t):
                    ctx.mark(g.dma_start(
                        out=xres[t * 128:(t + 1) * 128, :], in_=eres[:]), "dms", 16)
        if dbg:
            ctx.fence(["g"])
            @sect(ctx, "g")
            def _(g):
                ctx.mark(g.dma_start(out=dbg_sc[:, 0:16], in_=scal[:, :]), "dms", 16)
                ctx.mark(g.dma_start(out=dbg_sc[0:1, 16:32], in_=sc1[:, :]), "dms", 16)
                ctx.mark(g.dma_start(out=dbg_up[:, :], in_=ccbufs["sc_up"][0:128, :]), "dms", 16)
                ctx.mark(g.dma_start(out=dbg_g[:, :], in_=ccbufs["sc_g"][0:128, :]), "dms", 16)
                ctx.mark(g.dma_start(out=dbg_w[:, :], in_=wdnf[:, 0:3072]), "dms", 16)
                ctx.mark(g.dma_start(out=dbg_g2[:, :], in_=ccbufs["sc_g2"][0:128, :]), "dms", 16)
                ctx.mark(g.dma_start(out=dbg_o[:, :], in_=ccbufs["sc_o"][0:128, :]), "dms", 16)
                g.wait_ge(ctx.sem["dms"], ctx.tot["dms"])


def emit_l3_attn(ctx, xres, W):
    nc = ctx.nc
    es = contextlib.ExitStack()
    with es:
        ident = es.enter_context(nc.sbuf_tensor(ctx.nm("a3_id"), [128, 128], BF16))
        wfb = es.enter_context(nc.sbuf_tensor(ctx.nm("a3_wf"), [128, 64], BF16))
        bf_pc = es.enter_context(nc.sbuf_tensor(ctx.nm("a3_bf"), [128, 1], F32))
        bout = es.enter_context(nc.sbuf_tensor(ctx.nm("a3_bo"), [128, D], F32))
        onesb = es.enter_context(nc.sbuf_tensor(ctx.nm("a3_ones"), [128, 1], BF16))
        wstage = es.enter_context(nc.sbuf_tensor(ctx.nm("a3_ws"), [128, D], F32))
        stats = es.enter_context(nc.sbuf_tensor(ctx.nm("a3_st"), [128, NT_SEQ * 8], F32))

        @sect(ctx, "y")
        def _(sync):
            ctx.mark(sync.dma_start(out=ident[:], in_=W["ident"][:, :]), "dml", 16)
            ctx.mark(sync.dma_start(out=wstage[:, 0:64], in_=W["L3a_wf"][:, :]), "dml", 16)
            ctx.mark(sync.dma_start(out=bf_pc[:, :], in_=W["L3a_bf_pc"][:, :]), "dml", 16)
            ctx.mark(sync.dma_start(out=bout[:], in_=W["L3a_bout_bc"][:, :]), "dml", 16)
        ctx.fence(["v"], ["dml"])
        @sect(ctx, "v")
        def _(v):
            ctx.mark(v.tensor_copy(wfb[:, :], wstage[:, 0:64]), "v")
        @sect(ctx, "v")
        def _(v):
            ctx.mark(v.memset(onesb[:], 1.0), "v")
        for s in range(NSEQ):
            _l3a_one_seq(ctx, s, xres, W, ident, wfb, bf_pc, bout,
                         onesb, wstage, stats)


def _l3a_one_seq(ctx, s, xres, W, ident, wfb, bf_pc, bout, onesb,
                 wstage, stats):
    nc = ctx.nc
    tok0 = s * SEQ
    es = contextlib.ExitStack()
    with es:
        q0c = es.enter_context(nc.sbuf_tensor(ctx.nm("a3_q0c"), [128, SEQ], BF16))
        ql = es.enter_context(nc.sbuf_tensor(ctx.nm("a3_ql"), [128, DC * SEQ], BF16))
        klf = es.enter_context(nc.sbuf_tensor(ctx.nm("a3_klf"), [128, SEQ], BF16))
        kl = es.enter_context(nc.sbuf_tensor(ctx.nm("a3_kl"), [128, NT_SEQ * D], BF16))
        vt = es.enter_context(nc.sbuf_tensor(ctx.nm("a3_vt"), [128, NT_SEQ * D], BF16))
        kvb = es.enter_context(nc.sbuf_tensor(ctx.nm("a3_kvb"), [128, 16 * 64], BF16))
        ksb = es.enter_context(nc.sbuf_tensor(ctx.nm("a3_ksb"), [128, 16], BF16))
        xt = es.enter_context(nc.sbuf_tensor(ctx.nm("a3_xt"), [128, D], F32))
        sq = es.enter_context(nc.sbuf_tensor(ctx.nm("a3_sq"), [128, D], F32))
        lnb = es.enter_context(nc.sbuf_tensor(ctx.nm("a3_lnb"), [128, D], BF16))
        att = es.enter_context(nc.sbuf_tensor(ctx.nm("a3_at"), [128, D], BF16))
        atf = es.enter_context(nc.sbuf_tensor(ctx.nm("a3_atf"), [128, D], BF16))
        row = es.enter_context(nc.sbuf_tensor(ctx.nm("a3_row"), [128, 4], F32))
        eres = es.enter_context(nc.sbuf_tensor(ctx.nm("a3_er"), [128, D], F32))
        pa = es.enter_context(nc.psum_tensor(ctx.nm("a3_pa"), [128, SEQ], F32))
        mm = es.enter_context(nc.psum_tensor(ctx.nm("a3_mm"), [128, D], F32))
        ptp = es.enter_context(nc.psum_tensor(ctx.nm("a3_ptp"), [128, 1024], BF16))

        es_bc = contextlib.ExitStack()
        lnf = es_bc.enter_context(nc.sbuf_tensor(ctx.nm("a3_lnf"), [128, DC * SEQ], BF16))
        wvp = es_bc.enter_context(nc.sbuf_tensor(ctx.nm("a3_wvp"), [128, DC * D], BF16))
        wcol = es_bc.enter_context(nc.sbuf_tensor(ctx.nm("a3_wcol"), [128, DC * 128], BF16))
        load_weight_bf16(ctx, W["L3a_wv_p"], wvp, wstage, DC, D)

        for t in range(NT_SEQ):
            st = stats[:, t * 8:(t + 1) * 8]
            ctx.fence(["y"], ["v", "s", "t"])
            @sect(ctx, "y")
            def _(sync, t=t):
                ctx.mark(sync.dma_start(
                    out=xt[:], in_=xres[tok0 + t * 128: tok0 + (t + 1) * 128, :]), "dml", 16)
            ctx.fence(["v", "s"], ["dml"])
            emit_ln_stats(ctx, xt[:], st, sq[:], rms=False)
            ctx.fence(["s"], ["v"])
            @sect(ctx, "s")
            def _(sc, st=st):
                ctx.mark(sc.activation(lnb[:], xt[:], AF.Identity,
                                       bias=st[:, 2:3], scale=st[:, 1:2]), "s")
            ctx.fence(["t"], ["s"])
            @sect(ctx, "t")
            def _(te):
                for c in range(DC):
                    inst = te.transpose(ptp[:, c * 128:(c + 1) * 128],
                                        lnb[:, c * 128:(c + 1) * 128], ident[:])
                ctx.mark(inst, "t")
            ctx.fence(["s"], ["t"])
            @sect(ctx, "s")
            def _(sc, t=t):
                for c in range(DC):
                    inst = sc.copy(lnf[:, c * SEQ + t * 128: c * SEQ + (t + 1) * 128],
                                   ptp[:, c * 128:(c + 1) * 128])
                ctx.mark(inst, "s")
            ctx.fence(["t"], ["s", "v"])
            @sect(ctx, "t")
            def _(te, t=t):
                for col in range(2):
                    for c in range(DC):
                        inst = te.matmul(
                            mm[:, col * 512:(col + 1) * 512],
                            lnf[:, c * SEQ + t * 128: c * SEQ + (t + 1) * 128],
                            wvp[:, c * D + col * 512: c * D + (col + 1) * 512],
                            start=(c == 0), stop=(c == DC - 1))
                ctx.mark(inst, "t")
            ctx.fence(["v"], ["t"])
            @sect(ctx, "v")
            def _(v, t=t):
                ctx.mark(v.tensor_copy(vt[:, t * D:(t + 1) * D], mm[:, :]), "v")

        ctx.fence(["t"], ["s", "v"])
        for oc in range(DC):
            for bi, base in ((0, 0), (1, D)):
                blk = bi * 8 + oc
                ctx.fence(["y"], ["v", "t"])
                @sect(ctx, "y")
                def _(sync, blk=blk):
                    ctx.mark(sync.dma_start(
                        out=wstage[:], in_=W["L3a_wqk_t"][blk, :, :]), "dml", 16)
                ctx.fence(["v"], ["dml"])
                @sect(ctx, "v")
                def _(v):
                    ctx.mark(v.tensor_copy(wcol[:, :], wstage[:]), "v")
                ctx.fence(["t"], ["v"])
                @sect(ctx, "t")
                def _(te, oc=oc, base=base):
                    for c in range(DC):
                        for col in range(4):
                            inst = te.matmul(
                                pa[:, col * 512:(col + 1) * 512],
                                wcol[:, c * 128:(c + 1) * 128],
                                lnf[:, c * SEQ + col * 512: c * SEQ + (col + 1) * 512],
                                start=(c == 0), stop=(c == DC - 1))
                    ctx.mark(inst, "t")
                ctx.fence(["s"], ["t"])
                @sect(ctx, "s")
                def _(sc):
                    ctx.mark(sc.copy(q0c[:, :], pa[:, :]), "s")
                ctx.fence(["t"], ["s"])
                for hh in range(2):
                    rb = 64 * hh
                    h = oc * 2 + hh
                    @sect(ctx, "t")
                    def _(te, rb=rb):
                        for col in range(4):
                            inst = te.matmul(pa[0:64, col * 512:(col + 1) * 512],
                                             wfb[rb:rb + 64, :],
                                             q0c[rb:rb + 64, col * 512:(col + 1) * 512],
                                             start=True, stop=True)
                        ctx.mark(inst, "t")
                    ctx.fence(["s"], ["t"])
                    if base == 0:
                        @sect(ctx, "s")
                        def _(sc, oc=oc, rb=rb):
                            ctx.mark(sc.activation(
                                ql[rb:rb + 64, oc * SEQ:(oc + 1) * SEQ],
                                pa[0:64, :], AF.Relu, bias=bf_pc[0:64, 0:1]), "s")
                        ctx.fence(["t"], ["s"])
                    else:
                        @sect(ctx, "s")
                        def _(sc):
                            ctx.mark(sc.activation(klf[0:64, :], pa[0:64, :], AF.Relu,
                                                   bias=bf_pc[0:64, 0:1]), "s")
                        ctx.fence(["t"], ["s"])
                        @sect(ctx, "t")
                        def _(te, rb=rb):
                            for kc in range(16):
                                inst = te.transpose(
                                    ptp[:, kc * 64:(kc + 1) * 64],
                                    klf[0:64, kc * 128:(kc + 1) * 128],
                                    ident[0:64, 0:64])
                            ctx.mark(inst, "t")
                        ctx.fence(["s"], ["t"])
                        @sect(ctx, "s")
                        def _(sc, h=h):
                            for kc in range(16):
                                inst = sc.copy(
                                    kl[:, kc * D + h * 64: kc * D + (h + 1) * 64],
                                    ptp[:, kc * 64:(kc + 1) * 64])
                            ctx.mark(inst, "s")
                        ctx.fence(["t"], ["s"])

        es_bc.close()
        wout = es.enter_context(nc.sbuf_tensor(ctx.nm("a3_woutS"), [128, DC * D], BF16))
        load_weight_bf16(ctx, W["L3a_wout"], wout, wstage, DC, D)

        ctx.fence(["t"], ["s", "v"])
        for h in range(16):
            @sect(ctx, "t")
            def _(te, h=h):
                for kc in range(NT_SEQ):
                    te.matmul(pa[0:64, 0:64],
                              kl[:, kc * D + h * 64: kc * D + (h + 1) * 64],
                              vt[:, kc * D + h * 64: kc * D + (h + 1) * 64],
                              start=(kc == 0), stop=(kc == NT_SEQ - 1))
                for kc in range(NT_SEQ):
                    inst = te.matmul(pa[0:64, 512:513],
                                     kl[:, kc * D + h * 64: kc * D + (h + 1) * 64],
                                     onesb[:, 0:1],
                                     start=(kc == 0), stop=(kc == NT_SEQ - 1))
                ctx.mark(inst, "t")
            ctx.fence(["s"], ["t"])
            @sect(ctx, "s")
            def _(sc, h=h):
                sc.copy(kvb[0:64, h * 64:(h + 1) * 64], pa[0:64, 0:64])
                ctx.mark(sc.copy(ksb[0:64, h:h + 1], pa[0:64, 512:513]), "s")
            ctx.fence(["t"], ["s"])

        ctx.fence(["g"], ["s", "t"])
        @sect(ctx, "g")
        def _(g):
            ctx.mark(g.dma_start(out=kvb[64:128, :], in_=kvb[0:64, :]), "dml", 16)
            ctx.mark(g.dma_start(out=ksb[64:128, :], in_=ksb[0:64, :]), "dml", 16)
        ctx.fence(["t"], ["dml"])

        for qt in range(NT_SEQ):
            for h in range(16):
                oc, rb = h // 2, 64 * (h % 2)
                ctx.fence(["t"], ["v", "s"])
                @sect(ctx, "t")
                def _(te, oc=oc, rb=rb, qt=qt, h=h):
                    lhs = ql[rb:rb + 64, oc * SEQ + qt * 128: oc * SEQ + (qt + 1) * 128]
                    te.matmul(mm[:, 0:64], lhs, kvb[rb:rb + 64, h * 64:(h + 1) * 64],
                              start=True, stop=True)
                    inst = te.matmul(mm[:, 512:513], lhs, ksb[rb:rb + 64, h:h + 1],
                                     start=True, stop=True)
                    ctx.mark(inst, "t")
                ctx.fence(["v"], ["t"])
                @sect(ctx, "v")
                def _(v):
                    v.tensor_scalar_add(row[:, 0:1], mm[:, 512:513], 1e-6)
                    v.drain()
                    ctx.mark(v.reciprocal(row[:, 1:2], row[:, 0:1]), "v")
                ctx.fence(["s"], ["v"])
                @sect(ctx, "s")
                def _(sc, h=h):
                    ctx.mark(sc.activation(att[:, h * 64:(h + 1) * 64], mm[:, 0:64],
                                           AF.Identity, scale=row[:, 1:2]), "s")
                ctx.fence(["t"], ["s"])
            @sect(ctx, "t")
            def _(te):
                for c in range(DC):
                    inst = te.transpose(ptp[:, c * 128:(c + 1) * 128],
                                        att[:, c * 128:(c + 1) * 128], ident[:])
                ctx.mark(inst, "t")
            ctx.fence(["s"], ["t"])
            @sect(ctx, "s")
            def _(sc):
                for c in range(DC):
                    inst = sc.copy(atf[:, c * 128:(c + 1) * 128],
                                   ptp[:, c * 128:(c + 1) * 128])
                ctx.mark(inst, "s")
            ctx.fence(["y"], ["s", "g", "dms"])
            @sect(ctx, "y")
            def _(sync, qt=qt):
                ctx.mark(sync.dma_start(
                    out=xt[:], in_=xres[tok0 + qt * 128: tok0 + (qt + 1) * 128, :]), "dml", 16)
            ctx.fence(["t"], ["s", "v"])
            @sect(ctx, "t")
            def _(te):
                for c in range(DC):
                    for col in range(2):
                        inst = te.matmul(mm[:, col * 512:(col + 1) * 512],
                                         atf[:, c * 128:(c + 1) * 128],
                                         wout[:, c * D + col * 512: c * D + (col + 1) * 512],
                                         start=(c == 0), stop=(c == DC - 1))
                ctx.mark(inst, "t")
            ctx.fence(["v"], ["t", "dml", "dms"])
            @sect(ctx, "v")
            def _(v):
                v.tensor_add(eres[:], mm[:, :], bout[:])
                v.drain()
                ctx.mark(v.tensor_add(eres[:], eres[:], xt[:]), "v")
            ctx.fence(["g"], ["v"])
            @sect(ctx, "g")
            def _(g, qt=qt):
                ctx.mark(g.dma_start(
                    out=xres[tok0 + qt * 128: tok0 + (qt + 1) * 128, :], in_=eres[:]), "dms", 16)


def emit_l3_ffn(ctx, xres, W, ccbufs, dbg=False):
    nc = ctx.nc
    if dbg:
        dbg_z = nc.dram_tensor("dbg_z", [256, D], F32, kind="ExternalOutput").ap()
        dbg_y2 = nc.dram_tensor("dbg_y2", [256, D], F32, kind="ExternalOutput").ap()
    es = contextlib.ExitStack()
    with es:
        ident = es.enter_context(nc.sbuf_tensor(ctx.nm("f3_id"), [128, 128], BF16))
        stats = es.enter_context(nc.sbuf_tensor(ctx.nm("f3_st"), [128, NT_SEQ * 8], F32))
        mk = es.enter_context(nc.sbuf_tensor(ctx.nm("f3_mk"), [128, 32], F32))
        scl = es.enter_context(nc.sbuf_tensor(ctx.nm("f3_scl"), [128, 2 * NT_SEQ * 8], F32))
        bqs = es.enter_context(nc.sbuf_tensor(ctx.nm("f3_bqs"), [128, 8], F32))
        bks = es.enter_context(nc.sbuf_tensor(ctx.nm("f3_bks"), [128, 8], F32))
        bv_bc = es.enter_context(nc.sbuf_tensor(ctx.nm("f3_bv"), [128, D], F32))
        bo_bc = es.enter_context(nc.sbuf_tensor(ctx.nm("f3_bo"), [128, D], F32))
        l1g = es.enter_context(nc.sbuf_tensor(ctx.nm("f3_l1g"), [128, D], F32))
        l1b = es.enter_context(nc.sbuf_tensor(ctx.nm("f3_l1b"), [128, D], F32))
        l2g = es.enter_context(nc.sbuf_tensor(ctx.nm("f3_l2g"), [128, D], F32))
        l2b = es.enter_context(nc.sbuf_tensor(ctx.nm("f3_l2b"), [128, D], F32))
        gbb = es.enter_context(nc.sbuf_tensor(ctx.nm("f3_gbb"), [128, 8], F32))
        gate_w = es.enter_context(nc.sbuf_tensor(ctx.nm("f3_gw"), [128, DC * 8], BF16))
        b1pc = es.enter_context(nc.sbuf_tensor(ctx.nm("f3_b1"), [128, 256], F32))
        wstage = es.enter_context(nc.sbuf_tensor(ctx.nm("f3_ws"), [128, D], F32))
        xt = es.enter_context(nc.sbuf_tensor(ctx.nm("f3_xt"), [128, D], F32))
        sq = es.enter_context(nc.sbuf_tensor(ctx.nm("f3_sq"), [128, D], F32))
        tb = es.enter_context(nc.sbuf_tensor(ctx.nm("f3_tb"), [128, D], F32))
        eres = es.enter_context(nc.sbuf_tensor(ctx.nm("f3_er"), [128, D], F32))

        @sect(ctx, "y")
        def _(sync):
            ctx.mark(sync.dma_start(out=ident[:], in_=W["ident"][:, :]), "dml", 16)
            for name, buf in (("L3f_bq_pc", bqs), ("L3f_bk_pc", bks),
                              ("L3f_bv_bc", bv_bc), ("L3f_bo_bc", bo_bc),
                              ("L3f_ln1g_bc", l1g), ("L3f_ln1b_bc", l1b),
                              ("L3f_ln2g_bc", l2g), ("L3f_ln2b_bc", l2b),
                              ("L3f_gateb_bc", gbb), ("L3f_b1_pc", b1pc)):
                ctx.mark(sync.dma_start(out=buf[:], in_=W[name][:, :]), "dml", 16)
        ctx.fence(["v"], ["dml"])
        @sect(ctx, "v")
        def _(v):
            v.tensor_scalar_mul(bqs[:], bqs[:], 0.125)
            ctx.mark(v.engine_nop(), "v")
        load_weight_bf16(ctx, W["L3f_gate"], gate_w, wstage, DC, 8)

        for s in range(NSEQ):
            _l3f_attn_seq(ctx, s, xres, W, ccbufs, ident, stats, mk, bqs, bks,
                          bv_bc, bo_bc, l1g, l1b, wstage, xt, sq, tb, eres)
        for s in range(NSEQ):
            for hf in range(2):
                _l3f_moe_half(ctx, s, hf, xres, W, ccbufs, ident, gate_w, gbb,
                              b1pc, scl, mk, wstage, xt, sq, tb, eres, l2g, l2b)
        if dbg:
            ctx.fence(["g"])
            @sect(ctx, "g")
            def _(g):
                ctx.mark(g.dma_start(out=dbg_z[:, :], in_=ccbufs["zres"][0:256, :]), "dms", 16)
                ctx.mark(g.dma_start(out=dbg_y2[:, :], in_=ccbufs["y2acc"][0:256, :]), "dms", 16)
                g.wait_ge(ctx.sem["dms"], ctx.tot["dms"])


def _l3f_attn_seq(ctx, s, xres, W, ccbufs, ident, stats, mk, bqs, bks, bv_bc,
                  bo_bc, l1g, l1b, wstage, xt, sq, tb, eres):
    nc = ctx.nc
    tok0 = s * SEQ
    es = contextlib.ExitStack()
    with es:
        xinf = es.enter_context(nc.sbuf_tensor(ctx.nm("f3_xinf"), [128, DC * SEQ], BF16))
        qf = es.enter_context(nc.sbuf_tensor(ctx.nm("f3_qf"), [128, DC * SEQ], BF16))
        kf = es.enter_context(nc.sbuf_tensor(ctx.nm("f3_kf"), [128, DC * SEQ], BF16))
        vt = es.enter_context(nc.sbuf_tensor(ctx.nm("f3_vt"), [128, NT_SEQ * D], BF16))
        lnb = es.enter_context(nc.sbuf_tensor(ctx.nm("f3_lnb"), [128, D], BF16))
        P = es.enter_context(nc.sbuf_tensor(ctx.nm("f3_P"), [128, SEQ], BF16))
        Pt = es.enter_context(nc.sbuf_tensor(ctx.nm("f3_Pt"), [128, SEQ], BF16))
        attn = es.enter_context(nc.sbuf_tensor(ctx.nm("f3_at"), [128, D], BF16))
        atf = es.enter_context(nc.sbuf_tensor(ctx.nm("f3_atf"), [128, D], BF16))
        row = es.enter_context(nc.sbuf_tensor(ctx.nm("f3_row"), [128, 4], F32))
        wqk = es.enter_context(nc.sbuf_tensor(ctx.nm("f3_wqk"), [128, DC * D], BF16))
        S = es.enter_context(nc.psum_tensor(ctx.nm("f3_S"), [128, SEQ], F32))
        mm = es.enter_context(nc.psum_tensor(ctx.nm("f3_mm"), [128, D], F32))
        ptp = es.enter_context(nc.psum_tensor(ctx.nm("f3_ptp"), [128, 1024], BF16))

        load_weight_bf16(ctx, W["L3f_wv"], wqk, wstage, DC, D)

        for t in range(NT_SEQ):
            st = stats[:, t * 8:(t + 1) * 8]
            ctx.fence(["y"], ["v", "s", "t"])
            @sect(ctx, "y")
            def _(sync, t=t):
                ctx.mark(sync.dma_start(
                    out=xt[:], in_=xres[tok0 + t * 128: tok0 + (t + 1) * 128, :]), "dml", 16)
            ctx.fence(["v", "s"], ["dml"])
            emit_ln_stats(ctx, xt[:], st, sq[:], rms=False)
            ctx.fence(["s"], ["v"])
            @sect(ctx, "s")
            def _(sc, st=st):
                ctx.mark(sc.activation(lnb[:], xt[:], AF.Identity,
                                       bias=st[:, 2:3], scale=st[:, 1:2]), "s")
            ctx.fence(["t"], ["s"])
            @sect(ctx, "t")
            def _(te):
                for c in range(DC):
                    inst = te.transpose(ptp[:, c * 128:(c + 1) * 128],
                                        lnb[:, c * 128:(c + 1) * 128], ident[:])
                ctx.mark(inst, "t")
            ctx.fence(["s"], ["t"])
            @sect(ctx, "s")
            def _(sc, t=t):
                for c in range(DC):
                    inst = sc.copy(xinf[:, c * SEQ + t * 128: c * SEQ + (t + 1) * 128],
                                   ptp[:, c * 128:(c + 1) * 128])
                ctx.mark(inst, "s")
            ctx.fence(["t"], ["s", "v"])
            @sect(ctx, "t")
            def _(te, t=t):
                for col in range(2):
                    for c in range(DC):
                        inst = te.matmul(
                            mm[:, col * 512:(col + 1) * 512],
                            xinf[:, c * SEQ + t * 128: c * SEQ + (t + 1) * 128],
                            wqk[:, c * D + col * 512: c * D + (col + 1) * 512],
                            start=(c == 0), stop=(c == DC - 1))
                ctx.mark(inst, "t")
            ctx.fence(["v"], ["t"])
            @sect(ctx, "v")
            def _(v, t=t):
                ctx.mark(v.tensor_add(vt[:, t * D:(t + 1) * D], mm[:, :], bv_bc[:]), "v")

        for which, wname, dstbuf, bias in (("q", "L3f_wq", qf, bqs),
                                           ("k", "L3f_wk", kf, bks)):
            load_weight_bf16(ctx, W[wname], wqk, wstage, DC, D)
            ctx.fence(["t"], ["s", "v"])
            for oc in range(DC):
                @sect(ctx, "t")
                def _(te, oc=oc):
                    for c in range(DC):
                        for col in range(4):
                            inst = te.matmul(
                                S[:, col * 512:(col + 1) * 512],
                                wqk[:, c * D + oc * 128: c * D + (oc + 1) * 128],
                                xinf[:, c * SEQ + col * 512: c * SEQ + (col + 1) * 512],
                                start=(c == 0), stop=(c == DC - 1))
                    ctx.mark(inst, "t")
                ctx.fence(["s"], ["t"])
                @sect(ctx, "s")
                def _(sc, which=which, dstbuf=dstbuf, bias=bias, oc=oc):
                    scl_ = 0.125 if which == "q" else 1.0
                    inst = sc.activation(dstbuf[:, oc * SEQ:(oc + 1) * SEQ], S[:, :],
                                         AF.Identity, scale=scl_, bias=bias[:, oc:oc + 1])
                    ctx.mark(inst, "s")
                ctx.fence(["t"], ["s"])

        load_weight_bf16(ctx, W["L3f_wo"], wqk, wstage, DC, D)
        for qt in range(NT_SEQ):
            for h in range(16):
                oc, rb = h // 2, 64 * (h % 2)
                ctx.fence(["t"], ["v", "s"])
                @sect(ctx, "t")
                def _(te, oc=oc, rb=rb, qt=qt):
                    lhs = qf[rb:rb + 64, oc * SEQ + qt * 128: oc * SEQ + (qt + 1) * 128]
                    for col in range(4):
                        inst = te.matmul(
                            S[:, col * 512:(col + 1) * 512], lhs,
                            kf[rb:rb + 64, oc * SEQ + col * 512: oc * SEQ + (col + 1) * 512],
                            start=True, stop=True)
                    ctx.mark(inst, "t")
                ctx.fence(["v"], ["t"])
                @sect(ctx, "v")
                def _(v):
                    v.reduce_max(row[:, 0:1], S[:], axis=AX.X)
                    v.drain()
                    ctx.mark(v.tensor_scalar_mul(row[:, 1:2], row[:, 0:1], -1.0), "v")
                ctx.fence(["s"], ["v"])
                @sect(ctx, "s")
                def _(sc):
                    ctx.mark(sc.activation(P[:], S[:], AF.Exp, bias=row[:, 1:2],
                                           accum_out=row[:, 2:3]), "s")
                ctx.fence(["v"], ["s"])
                @sect(ctx, "v")
                def _(v):
                    ctx.mark(v.reciprocal(row[:, 3:4], row[:, 2:3]), "v")
                for b in range(2):
                    ctx.fence(["t"], ["s", "v"])
                    @sect(ctx, "t")
                    def _(te, b=b):
                        for k in range(8):
                            kc = b * 8 + k
                            inst = te.transpose(ptp[:, k * 128:(k + 1) * 128],
                                                P[:, kc * 128:(kc + 1) * 128], ident[:])
                        ctx.mark(inst, "t")
                    ctx.fence(["s"], ["t"])
                    @sect(ctx, "s")
                    def _(sc, b=b):
                        for k in range(8):
                            kc = b * 8 + k
                            inst = sc.copy(Pt[:, kc * 128:(kc + 1) * 128],
                                           ptp[:, k * 128:(k + 1) * 128])
                        ctx.mark(inst, "s")
                ctx.fence(["t"], ["s"])
                @sect(ctx, "t")
                def _(te, h=h):
                    for kc in range(NT_SEQ):
                        inst = te.matmul(mm[:, 0:64], Pt[:, kc * 128:(kc + 1) * 128],
                                         vt[:, kc * D + h * 64: kc * D + (h + 1) * 64],
                                         start=(kc == 0), stop=(kc == NT_SEQ - 1))
                    ctx.mark(inst, "t")
                ctx.fence(["s"], ["t", "v"])
                @sect(ctx, "s")
                def _(sc, h=h):
                    ctx.mark(sc.activation(attn[:, h * 64:(h + 1) * 64], mm[:, 0:64],
                                           AF.Identity, scale=row[:, 3:4]), "s")
                ctx.fence(["t"], ["s"])

            ctx.fence(["t"], ["s"])
            @sect(ctx, "t")
            def _(te):
                for c in range(DC):
                    inst = te.transpose(ptp[:, c * 128:(c + 1) * 128],
                                        attn[:, c * 128:(c + 1) * 128], ident[:])
                ctx.mark(inst, "t")
            ctx.fence(["s"], ["t"])
            @sect(ctx, "s")
            def _(sc):
                for c in range(DC):
                    inst = sc.copy(atf[:, c * 128:(c + 1) * 128],
                                   ptp[:, c * 128:(c + 1) * 128])
                ctx.mark(inst, "s")
            ctx.fence(["y"], ["s", "g", "dms"])
            @sect(ctx, "y")
            def _(sync, qt=qt):
                ctx.mark(sync.dma_start(
                    out=xt[:], in_=xres[tok0 + qt * 128: tok0 + (qt + 1) * 128, :]), "dml", 16)
            ctx.fence(["t"], ["s", "v"])
            @sect(ctx, "t")
            def _(te):
                for c in range(DC):
                    for col in range(2):
                        inst = te.matmul(mm[:, col * 512:(col + 1) * 512],
                                         atf[:, c * 128:(c + 1) * 128],
                                         wqk[:, c * D + col * 512: c * D + (col + 1) * 512],
                                         start=(c == 0), stop=(c == DC - 1))
                ctx.mark(inst, "t")
            ctx.fence(["v"], ["t", "dml", "dms"])
            @sect(ctx, "v")
            def _(v):
                ctx.mark(v.tensor_add(tb[:], mm[:, :], bo_bc[:]), "v")
            ctx.fence(["s", "v"], ["v"])
            emit_ln_stats(ctx, tb[:], mk[:, 0:8], sq[:], rms=False)
            ctx.fence(["s"], ["v"])
            @sect(ctx, "s")
            def _(sc, qt=qt):
                ctx.mark(sc.activation(sq[:], tb[:], AF.Identity,
                                       bias=mk[:, 2:3], scale=mk[:, 1:2]), "s")
            st = stats[:, qt * 8:(qt + 1) * 8]
            ctx.fence(["v"], ["s"])
            @sect(ctx, "v")
            def _(v, st=st):
                v.tensor_mul(sq[:], sq[:], l1g[:])
                v.drain()
                v.tensor_add(sq[:], sq[:], l1b[:])
                ctx.mark(v.engine_nop(), "v")
            @sect(ctx, "s")
            def _(sc, st=st):
                ctx.mark(sc.activation(eres[:], xt[:], AF.Identity,
                                       bias=st[:, 2:3], scale=st[:, 1:2]), "s")
            ctx.fence(["v"], ["s"])
            @sect(ctx, "v")
            def _(v):
                v.drain()
                ctx.mark(v.tensor_add(eres[:], eres[:], sq[:]), "v")
            ctx.fence(["g"], ["v"])
            @sect(ctx, "g")
            def _(g, qt=qt):
                ctx.mark(g.dma_start(
                    out=ccbufs["zres"][tok0 + qt * 128: tok0 + (qt + 1) * 128, :],
                    in_=eres[:]), "dms", 16)


def _l3f_moe_half(ctx, s, hf, xres, W, ccbufs, ident, gate_w, gbb, b1pc, scl,
                  mk, wstage, xt, sq, tb, eres, l2g, l2b):
    nc = ctx.nc
    tok0 = s * SEQ + hf * 1024
    HT = 8
    es = contextlib.ExitStack()
    with es:
        zf = es.enter_context(nc.sbuf_tensor(ctx.nm("f3_zf"), [128, DC * 1024], BF16))
        uf = es.enter_context(nc.sbuf_tensor(ctx.nm("f3_uf"), [128, 32 * 1024], BF16))
        w2r = es.enter_context(nc.sbuf_tensor(ctx.nm("f3_w2r"), [128, 32 * 1024], BF16))
        w1s = es.enter_context(nc.sbuf_tensor(ctx.nm("f3_w1s"), [128, 2 * 1024], F32))
        w1b = es.enter_context(nc.sbuf_tensor(ctx.nm("f3_w1b"), [128, 1024], BF16))
        b2s = es.enter_context(nc.sbuf_tensor(ctx.nm("f3_b2s"), [128, D], F32))
        zb = es.enter_context(nc.sbuf_tensor(ctx.nm("f3_zb"), [128, D], BF16))
        parena = es.enter_context(nc.psum_tensor(ctx.nm("f3_pa"), [128, 3 * D], F32))
        ptp = es.enter_context(nc.psum_tensor(ctx.nm("f3_ptp2"), [128, 1024], BF16))

        for t in range(HT):
            gt = s * NT_SEQ * 8 + (hf * HT + t) * 8
            ctx.fence(["y"], ["v", "s", "t", "dms"])
            @sect(ctx, "y")
            def _(sync, t=t):
                ctx.mark(sync.dma_start(
                    out=xt[:], in_=ccbufs["zres"][tok0 + t * 128: tok0 + (t + 1) * 128, :]),
                    "dml", 16)
            ctx.fence(["v"], ["dml"])
            @sect(ctx, "v")
            def _(v):
                ctx.mark(v.tensor_copy(zb[:], xt[:]), "v")
            ctx.fence(["t"], ["v", "s"])
            @sect(ctx, "t")
            def _(te):
                for c in range(DC):
                    inst = te.transpose(ptp[:, c * 128:(c + 1) * 128],
                                        zb[:, c * 128:(c + 1) * 128], ident[:])
                ctx.mark(inst, "t")
            ctx.fence(["s"], ["t"])
            @sect(ctx, "s")
            def _(sc, t=t):
                for c in range(DC):
                    inst = sc.copy(zf[:, c * 1024 + t * 128: c * 1024 + (t + 1) * 128],
                                   ptp[:, c * 128:(c + 1) * 128])
                ctx.mark(inst, "s")
            ctx.fence(["t"], ["s"])
            @sect(ctx, "t")
            def _(te, t=t):
                for c in range(DC):
                    inst = te.matmul(parena[:, 0:8],
                                     zf[:, c * 1024 + t * 128: c * 1024 + (t + 1) * 128],
                                     gate_w[:, c * 8:(c + 1) * 8],
                                     start=(c == 0), stop=(c == DC - 1))
                ctx.mark(inst, "t")
            ctx.fence(["v"], ["t"])
            @sect(ctx, "v")
            def _(v, gt=gt):
                v.tensor_add(mk[:, 8:16], parena[:, 0:8], gbb[:])
                v.drain()
                v.reduce_max(mk[:, 0:1], mk[:, 8:16], axis=AX.X)
                v.drain()
                ctx.mark(v.tensor_scalar_mul(mk[:, 1:2], mk[:, 0:1], -1.0), "v")
            ctx.fence(["s"], ["v"])
            @sect(ctx, "s")
            def _(sc):
                ctx.mark(sc.activation(mk[:, 8:16], mk[:, 8:16], AF.Exp,
                                       bias=mk[:, 1:2], accum_out=mk[:, 2:3]), "s")
            ctx.fence(["v"], ["s"])
            @sect(ctx, "v")
            def _(v, gt=gt):
                v.reciprocal(mk[:, 3:4], mk[:, 2:3])
                v.drain()
                v.tensor_scalar(mk[:, 8:16], mk[:, 8:16], mk[:, 3:4], None, ALU.mult)
                v.drain()
                v.tensor_reduce(mk[:, 4:5], mk[:, 8:16], axis=AX.X, op=ALU.max)
                v.drain()
                m1b_ = mk[:, 4:5].broadcast_to([128, 8])
                v.tensor_tensor(mk[:, 16:24], mk[:, 8:16], m1b_, ALU.is_equal)
                v.drain()
                v.scalar_tensor_tensor(mk[:, 16:24], mk[:, 16:24], -1e30,
                                       mk[:, 8:16], ALU.mult, ALU.add)
                v.drain()
                v.tensor_reduce(mk[:, 4:5], mk[:, 16:24], axis=AX.X, op=ALU.max)
                v.drain()
                v.tensor_tensor(mk[:, 16:24], mk[:, 8:16], m1b_, ALU.is_ge)
                v.drain()
                inst = v.tensor_mul(scl[:, gt:gt + 8], mk[:, 16:24], mk[:, 8:16])
                ctx.mark(inst, "v")

        for e in range(8):
            for c in range(32):
                ctx.fence(["y"], ["v", "s", "t"])
                @sect(ctx, "y")
                def _(sync, e=e, c=c):
                    ctx.mark(sync.dma_start(
                        out=wstage[:], in_=W["L3f_w2"][e, c * 128:(c + 1) * 128, :]), "dml", 16)
                ctx.fence(["v"], ["dml"])
                @sect(ctx, "v")
                def _(v, c=c):
                    ctx.mark(v.tensor_copy(w2r[:, c * 1024:(c + 1) * 1024], wstage[:]), "v")
            @sect(ctx, "y")
            def _(sync, e=e):
                ctx.mark(sync.dma_start(out=b2s[:], in_=W["L3f_b2_bc"][:, e * D:(e + 1) * D]),
                         "dml", 16)

            for f in range(32):
                buf = w1s[:, (f % 2) * 1024:(f % 2 + 1) * 1024]
                ctx.fence(["y"], ["v"])
                @sect(ctx, "y")
                def _(sync, e=e, f=f, buf=buf):
                    ctx.mark(sync.dma_start(out=buf, in_=W["L3f_w1t"][e, f, :, :]), "dml", 16)
                ctx.fence(["v"], ["dml", "t"])
                @sect(ctx, "v")
                def _(v, buf=buf):
                    ctx.mark(v.tensor_copy(w1b[:], buf), "v")
                ctx.fence(["t"], ["v", "s"])
                @sect(ctx, "t")
                def _(te, f=f):
                    ps = parena[:, (f % 2) * D:(f % 2 + 1) * D]
                    for c in range(DC):
                        for col in range(2):
                            inst = te.matmul(
                                ps[:, col * 512:(col + 1) * 512],
                                w1b[:, c * 128:(c + 1) * 128],
                                zf[:, c * 1024 + col * 512: c * 1024 + (col + 1) * 512],
                                start=(c == 0), stop=(c == DC - 1))
                    ctx.mark(inst, "t")
                ctx.fence(["s"], ["t"])
                @sect(ctx, "s")
                def _(sc, e=e, f=f):
                    ps = parena[:, (f % 2) * D:(f % 2 + 1) * D]
                    inst = sc.activation(uf[:, f * 1024:(f + 1) * 1024], ps[:],
                                         AF.Gelu, bias=b1pc[:, e * 32 + f: e * 32 + f + 1])
                    ctx.mark(inst, "s")
                if f < 31:
                    ctx.fence(["t"], ["s"])

            for pair in range(4):
                ctx.fence(["t"], ["s", "v"])
                @sect(ctx, "t")
                def _(te, pair=pair):
                    for f in range(32):
                        for tq in range(2):
                            ht = pair * 2 + tq
                            for col in range(2):
                                inst = te.matmul(
                                    parena[:, tq * D + col * 512: tq * D + (col + 1) * 512],
                                    uf[:, f * 1024 + ht * 128: f * 1024 + (ht + 1) * 128],
                                    w2r[:, f * 1024 + col * 512: f * 1024 + (col + 1) * 512],
                                    start=(f == 0), stop=(f == 31))
                    ctx.mark(inst, "t")
                for tq in range(2):
                    ht = pair * 2 + tq
                    gt = s * NT_SEQ * 8 + (hf * HT + ht) * 8
                    ctx.fence(["v"], ["t", "dms", "dml"])
                    @sect(ctx, "v")
                    def _(v, tq=tq, gt=gt, e=e):
                        v.tensor_add(tb[:], parena[:, tq * D:(tq + 1) * D], b2s[:])
                        v.drain()
                        inst = v.tensor_scalar(tb[:], tb[:], scl[:, gt + e:gt + e + 1],
                                               None, ALU.mult)
                        ctx.mark(inst, "v")
                    ctx.fence(["g"], ["v"])
                    @sect(ctx, "g")
                    def _(g, ht=ht, e=e):
                        if e == 0:
                            inst = g.dma_start(
                                out=ccbufs["y2acc"][tok0 + ht * 128: tok0 + (ht + 1) * 128, :],
                                in_=tb[:])
                        else:
                            inst = g.dma_start(
                                out=ccbufs["y2acc"][tok0 + ht * 128: tok0 + (ht + 1) * 128, :],
                                in_=tb[:], accum_op=ALU.add)
                        ctx.mark(inst, "dms", 16)
                    ctx.fence(["v"], ["dms"])

        for t in range(HT):
            ctx.fence(["y"], ["v", "s", "g", "dms"])
            @sect(ctx, "y")
            def _(sync, t=t):
                ctx.mark(sync.dma_start(
                    out=xt[:], in_=ccbufs["y2acc"][tok0 + t * 128: tok0 + (t + 1) * 128, :]),
                    "dml", 16)
                ctx.mark(sync.dma_start(
                    out=tb[:], in_=ccbufs["zres"][tok0 + t * 128: tok0 + (t + 1) * 128, :]),
                    "dml", 16)
                ctx.mark(sync.dma_start(
                    out=eres[:], in_=xres[tok0 + t * 128: tok0 + (t + 1) * 128, :]),
                    "dml", 16)
            ctx.fence(["v", "s"], ["dml"])
            emit_ln_stats(ctx, xt[:], mk[:, 0:8], sq[:], rms=False)
            ctx.fence(["s"], ["v"])
            @sect(ctx, "s")
            def _(sc):
                ctx.mark(sc.activation(sq[:], xt[:], AF.Identity,
                                       bias=mk[:, 2:3], scale=mk[:, 1:2]), "s")
            ctx.fence(["v"], ["s"])
            @sect(ctx, "v")
            def _(v):
                v.tensor_mul(sq[:], sq[:], l2g[:])
                v.drain()
                v.tensor_add(sq[:], sq[:], l2b[:])
                v.drain()
                v.tensor_add(sq[:], sq[:], tb[:])
                v.drain()
                ctx.mark(v.tensor_add(eres[:], eres[:], sq[:]), "v")
            ctx.fence(["g"], ["v"])
            @sect(ctx, "g")
            def _(g, t=t):
                ctx.mark(g.dma_start(
                    out=xres[tok0 + t * 128: tok0 + (t + 1) * 128, :], in_=eres[:]), "dms", 16)
